# version 12
# baseline (speedup 1.0000x reference)
import math
import numpy as np
import concourse.bass as bass
import concourse.mybir as mybir
from concourse.alu_op_type import AluOpType
from concourse.bass_utils import run_bass_kernel_spmd
from contextlib import ExitStack

F32 = mybir.dt.float32
BF16 = mybir.dt.bfloat16
AX = mybir.AxisListType.X
AF = mybir.ActivationFunctionType
Alu = AluOpType

S, D, H, HD, HID = 4096, 2048, 16, 128, 8192
NQ = 2048
NEG = -1.0e30
EPS = 1e-5

_S32 = np.float32(np.sqrt(np.float32(128.0)))
_C64 = 1.0 / np.float64(_S32)
C_HI = float(np.float32(_C64))
C_LO = float(np.float32(_C64 - np.float64(np.float32(_C64))))
_GC64 = math.sqrt(2.0 / math.pi)
GC_HI = float(np.float32(_GC64))
GC_LO = float(np.float32(_GC64 - np.float64(np.float32(_GC64))))
_GA64 = 0.044715
GA_HI = float(np.float32(_GA64))
GA_LO = float(np.float32(_GA64 - np.float64(np.float32(_GA64))))
_C164 = _GC64 * _GA64
C1_HI = float(np.float32(_C164))
C1_LO = float(np.float32(_C164 - np.float64(np.float32(_C164))))


def qperm_rows(g):
    rows = []
    for i in range(16):
        v, c01 = i // 2, i % 2
        qj = 4 * v + 2 * g + c01
        rows.extend(range(128 * qj, 128 * qj + 128))
    return np.array(rows)


_SLOPE_BITS = np.array([
    0x3f3504f3, 0x3effffff, 0x3eb504f3, 0x3e7fffff,
    0x3e3504f2, 0x3dfffffe, 0x3db504f2, 0x3d7ffffe,
    0x3d3504f1, 0x3cfffffd, 0x3cb504f1, 0x3c7ffffd,
    0x3c3504f1, 0x3bfffffc, 0x3bb504f0, 0x3b7ffffb], dtype=np.uint32)


def alibi_table():
    slopes = _SLOPE_BITS.view(np.float32)
    dist = np.arange(1 - S, 1, dtype=np.float32)
    tab = (slopes[:, None] * dist[None, :]).astype(np.float32)
    return np.ascontiguousarray(np.broadcast_to(tab[:, None, :], (16, 128, S))).copy()


def mask_consts(g):
    out = np.full((2, 128, 512), NEG, np.float32)
    for c01 in range(2):
        r = 2 * g + c01
        for q in range(128):
            out[c01, q, :128 * r + q + 1] = 0.0
    return out


def build_program(phases_upto=99):
    nc = bass.Bass("TRN2", target_bir_lowering=False, debug=False,
                   detect_race_conditions=False)

    def din(name, shape, dt=F32):
        return nc.dram_tensor(name, shape, dt, kind="ExternalInput").ap()

    x_full = din("x_full", [S, D])
    x_q = din("x_q", [NQ, D])
    Wq = din("Wq", [D, D]); Wk = din("Wk", [D, D])
    Wv = din("Wv", [D, D]); Wo = din("Wo", [D, D])
    bq_t = din("bq_t", [128, 16]); bk_t = din("bk_t", [128, 16])
    bv_row = din("bv_row", [1, D]); bo_row = din("bo_row", [1, D])
    w1 = din("w1", [HID, D]); w2 = din("w2", [D, HID])
    alibi = din("alibi", [16, 128, S])
    maskcd = din("maskc", [2, 128, 512])
    identd = din("identd", [128, 128])
    o_q = nc.dram_tensor("o_q", [NQ, D], F32, kind="ExternalOutput").ap()

    def dscr(name, shape, dt=F32):
        return nc.dram_tensor(name, shape, dt, kind="Internal").ap()

    xhf_scr = [dscr("xhf_scr_h", [16, 128, S], BF16), dscr("xhf_scr_l", [16, 128, S], BF16)]
    xhq_scr = [dscr("xhq_scr_h", [16, 128, NQ], BF16), dscr("xhq_scr_l", [16, 128, NQ], BF16)]
    q_scr = [dscr("q_scr_h", [16, 128, NQ], BF16), dscr("q_scr_l", [16, 128, NQ], BF16)]
    k_scr = [dscr("k_scr_h", [16, 128, S], BF16), dscr("k_scr_l", [16, 128, S], BF16)]
    v_scr = [dscr("v_scr_h", [S, D], BF16), dscr("v_scr_l", [S, D], BF16)]
    ctxT_scr = [dscr("ctxT_scr_h", [16, D, 128], BF16), dscr("ctxT_scr_l", [16, D, 128], BF16)]
    x2_scr = dscr("x2_scr", [NQ, D])
    v1_scr = dscr("v1_scr", [16, 128, HID])
    hqT_scr = dscr("hqT_scr", [16, 64, 128, 128], BF16)

    cnt = {}
    lab = {}

    def _inc(s, n):
        cnt[s] = cnt.get(s, 0) + n
        return cnt[s]

    with ExitStack() as outer:
        def sbT(st, name, shape, dt=F32):
            return st.enter_context(nc.sbuf_tensor(name, shape, dt))

        def psT(st, name, shape, dt=F32):
            return st.enter_context(nc.psum_tensor(name, shape, dt))

        sem = {}
        for sname in ("sLD", "sGP", "sSC", "sPE", "sDV", "sAC", "sDVt", "sACt",
                      "sGPc", "sBAR"):
            sem[sname] = outer.enter_context(nc.semaphore(sname))

        block = outer.enter_context(nc.Block())
        ENG = {"sync": block.sync, "tensor": block.tensor, "vector": block.vector,
               "scalar": block.scalar, "gpsimd": block.gpsimd}

        def mark(inst, s, label=None, n=1):
            inst.then_inc(sem[s], n)
            v = _inc(s, n)
            if label is not None:
                lab[label] = (s, v)
            return v

        def wt(e, label):
            s, v = lab[label]
            e.wait_ge(sem[s], v)

        def wto(e, label):
            if label in lab:
                wt(e, label)

        def dmax(e, q, out, in_, label=None):
            mark(e.dma_start(out=out, in_=in_), q, label, 16)

        def tick(e, inst, s="sDVt"):
            mark(inst, s)
            e.wait_ge(sem[s], cnt[s])

        def barrier():
            n = _inc("bar", 1)
            for ename, dec in ENG.items():
                @dec
                def _(e):
                    e.sem_inc(sem["sBAR"], 1)
                    e.wait_ge(sem["sBAR"], 5 * n)

        ident = sbT(outer, "ident", [128, 128])
        identb = sbT(outer, "identb", [128, 128], BF16)
        bqc = sbT(outer, "bqc", [128, 16])
        bkc = sbT(outer, "bkc", [128, 16])
        cols = sbT(outer, "cols", [128, 576])
        ones1 = sbT(outer, "ones1", [1, 128])

        CO = dict(sum1=0, sq1=32, mean1=64, rstd1=96, var1=128,
                  sumq=160, sqq=176, meanq=192, rstdq=208, varq=224,
                  sum2=240, sq2=256, mean2=272, rstd2=288, var2=304,
                  sclx=320, rsx1n=336, ghmax=352, s05=368, rs2neg=384,
                  swq=400, rsum=404, nmax=405, rnorm=406, tm=407,
                  tmpa=416, tmpb=448, tmpc=480, tmpd=512)

        def col(slot, w=1, off=0):
            return cols[:, CO[slot] + off:CO[slot] + off + w]


        @block.sync
        def _(sy):
            dmax(sy, "sLD", ident[:], identd[:])
            dmax(sy, "sLD", bqc[:], bq_t[:])
            dmax(sy, "sLD", bkc[:], bk_t[:], label="p0_ld")
        p0_ld = cnt["sLD"]

        @block.vector
        def _(vec):
            vec.wait_ge(sem["sLD"], p0_ld)
            vec.tensor_copy(out=identb[:], in_=ident[:])
            mark(vec.memset(ones1[:], 1.0), "sDV", "p0_ones")

        def emit_bcast_row(tag, tgt, row_ap, pbc):
            @block.sync
            def _(sy):
                dmax(sy, "sLD", tgt[0:1, 0:D], row_ap, label=f"{tag}_row")
            for j in range(4):
                @block.tensor
                def _(te, j=j):
                    wt(te, "p0_ones")
                    wt(te, f"{tag}_row")
                    wto(te, f"{tag}_bc{j - 1}")
                    mark(te.matmul(out=pbc[:], lhsT=ones1[:],
                                   rhs=tgt[0:1, 512 * j:512 * (j + 1)],
                                   start=True, stop=True), "sPE", f"{tag}_mm{j}")

                @block.vector
                def _(vec, j=j):
                    wt(vec, f"{tag}_mm{j}")
                    mark(vec.tensor_copy(out=tgt[:, 512 * j:512 * (j + 1)], in_=pbc[:]),
                         "sDV", f"{tag}_bc{j}")
        barrier()

        def p1a(tag, src_ap, ntiles, sl_sum, sl_sq):
            st = ExitStack()
            bufs = [sbT(st, f"{tag}_xa", [128, D]), sbT(st, f"{tag}_xb", [128, D])]
            junk = sbT(st, f"{tag}_junk", [128, D])
            for t in range(ntiles):
                @block.sync
                def _(sy, t=t):
                    if t >= 2:
                        wt(sy, f"{tag}_sq{t - 2}")
                        wt(sy, f"{tag}_sum{t - 2}")
                    dmax(sy, "sLD", bufs[t % 2][:],
                         src_ap[128 * t:128 * (t + 1), :], label=f"{tag}_x{t}")

                @block.scalar
                def _(sc, t=t):
                    wt(sc, f"{tag}_x{t}")
                    mark(sc.activation(out=junk[:], in_=bufs[t % 2][:], func=AF.Square,
                                       accum_out=col(sl_sq, 1, t)),
                         "sAC", f"{tag}_sq{t}")

                @block.vector
                def _(vec, t=t):
                    wt(vec, f"{tag}_x{t}")
                    mark(vec.reduce_sum(out=col(sl_sum, 1, t), in_=bufs[t % 2][:],
                                        axis=AX), "sDV", f"{tag}_sum{t}")
            st.close()

        def p1a_colmath(tag, n, sl_sum, sl_sq, sl_mean, sl_var, sl_rstd, inv_n,
                        wait_sq, wait_sum):
            @block.vector
            def _(vec):
                wt(vec, wait_sq)
                wt(vec, wait_sum)
                tick(vec, vec.tensor_scalar(out=col(sl_mean, n), in0=col(sl_sum, n),
                                            scalar1=inv_n, scalar2=None, op0=Alu.mult))
                tick(vec, vec.tensor_scalar(out=col("tmpa", n), in0=col(sl_sq, n),
                                            scalar1=inv_n, scalar2=None, op0=Alu.mult))
                tick(vec, vec.tensor_tensor(out=col("tmpb", n), in0=col(sl_mean, n),
                                            in1=col(sl_mean, n), op=Alu.mult))
                mark(vec.tensor_tensor(out=col(sl_var, n), in0=col("tmpa", n),
                                       in1=col("tmpb", n), op=Alu.subtract),
                     "sDV", f"{tag}_var")

            @block.vector
            def _(vec):
                wt(vec, f"{tag}_var")
                mark(vec.tensor_scalar(out=col("tmpd", n), in0=col(sl_var, n),
                                       scalar1=EPS, scalar2=None, op0=Alu.add),
                     "sDV", f"{tag}_vareps")

            @block.scalar
            def _(sc):
                wt(sc, f"{tag}_vareps")
                mark(sc.activation(out=col("tmpc", n), in_=col("tmpd", n),
                                   func=AF.Sqrt),
                     "sAC", f"{tag}_sd")

            @block.vector
            def _(vec):
                wt(vec, f"{tag}_sd")
                mark(vec.reciprocal(out=col(sl_rstd, n), in_=col("tmpc", n)),
                     "sDV", f"{tag}_rstd")

        if phases_upto >= 1:
            p1a("lf", x_full, 32, "sum1", "sq1")
            p1a("lq", x_q, 16, "sumq", "sqq")
            p1a_colmath("lf", 32, "sum1", "sq1", "mean1", "var1", "rstd1",
                        1.0 / 2048, "lf_sq31", "lf_sum31")
            p1a_colmath("lq", 16, "sumq", "sqq", "meanq", "varq", "rstdq",
                        1.0 / 2048, "lq_sq15", "lq_sum15")
            barrier()

        def p1b(tag, src_ap, dst_scr, ntiles, sl_mean, sl_rstd, rstd_lbl):
            st = ExitStack()
            bufs = [sbT(st, f"{tag}_a", [128, D]), sbT(st, f"{tag}_b", [128, D])]
            xh = sbT(st, f"{tag}_h", [128, D])
            xhl = [sbT(st, f"{tag}_s0", [128, D], BF16), sbT(st, f"{tag}_s1", [128, D], BF16)]
            fmg = [sbT(st, f"{tag}_fm0", [128, 8192], BF16), sbT(st, f"{tag}_fm1", [128, 8192], BF16)]
            pst = [psT(st, f"{tag}_p0", [128, 512], BF16), psT(st, f"{tag}_p1", [128, 512], BF16)]
            ngrp = ntiles // 4
            for grp in range(ngrp):
                for tl in range(4):
                    t = grp * 4 + tl

                    @block.sync
                    def _(sy, t=t):
                        if t >= 2:
                            wt(sy, f"{tag}_xh{t - 1}")
                        dmax(sy, "sLD", bufs[t % 2][:],
                             src_ap[128 * t:128 * (t + 1), :], label=f"{tag}_x{t}")

                    @block.vector
                    def _(vec, t=t, grp=grp, tl=tl):
                        wt(vec, f"{tag}_x{t}")
                        if t == 0:
                            wt(vec, rstd_lbl)
                        if tl == 0 and grp > 0:
                            wt(vec, f"{tag}_spill{grp - 1}")
                        vec.tensor_scalar(out=xh[:], in0=bufs[t % 2][:],
                                          scalar1=col(sl_mean, 1, t),
                                          scalar2=col(sl_rstd, 1, t),
                                          op0=Alu.subtract, op1=Alu.mult)
                        vec.tensor_copy(out=xhl[0][:], in_=xh[:])
                        mark(vec.tensor_tensor(out=xhl[1][:], in0=xh[:], in1=xhl[0][:],
                                               op=Alu.subtract),
                             "sDV", f"{tag}_xh{t}")

                    for hl in range(2):
                        for qd in range(4):
                            @block.tensor
                            def _(te, t=t, qd=qd, hl=hl):
                                wt(te, f"{tag}_xh{t}")
                                gq = _inc(f"{tag}_q", 1) - 1
                                if gq >= 2:
                                    wt(te, f"{tag}_fme{gq - 2}")
                                for u in range(4):
                                    dc = qd * 4 + u
                                    _li = te.transpose(
                                        out=pst[gq % 2][:, 128 * u:128 * (u + 1)],
                                        in_=xhl[hl][:, 128 * dc:128 * (dc + 1)],
                                        identity=identb[:])
                                mark(_li, "sPE", f"{tag}_tq{gq}")

                            @block.vector
                            def _(vec, t=t, qd=qd, tl=tl, hl=hl):
                                gq = _inc(f"{tag}_qe", 1) - 1
                                wt(vec, f"{tag}_tq{gq}")
                                fmr = fmg[hl][:].rearrange("p (dc tl c) -> p dc tl c",
                                                           dc=16, tl=4)
                                mark(vec.tensor_copy(
                                    out=fmr[:, 4 * qd:4 * qd + 4, tl, :],
                                    in_=pst[gq % 2][:].rearrange("p (u c) -> p u c", u=4)),
                                    "sDV", f"{tag}_fme{gq}")

                @block.gpsimd
                def _(gp, grp=grp):
                    wt(gp, f"{tag}_fme{_inc(f'{tag}_qe', 0) - 1}")
                    for hl in range(2):
                        for dc in range(16):
                            last = hl == 1 and dc == 15
                            dmax(gp, "sGP", dst_scr[hl][dc][:, 512 * grp:512 * (grp + 1)],
                                 fmg[hl][:, 512 * dc:512 * (dc + 1)],
                                 label=f"{tag}_spill{grp}" if last else None)
            st.close()

        if phases_upto >= 2:
            p1b("xf", x_full, xhf_scr, 32, "mean1", "rstd1", "lf_rstd")
            barrier()
            p1b("xq", x_q, xhq_scr, 16, "meanq", "rstdq", "lq_rstd")
            barrier()

        def load_w_split(tag, wdram, st):
            whb = sbT(st, f"{tag}_wh", [128, 32768], BF16)
            wlb = sbT(st, f"{tag}_wl", [128, 32768], BF16)
            wstg = [sbT(st, f"{tag}_ws0", [128, D]), sbT(st, f"{tag}_ws1", [128, D])]
            for dc in range(16):
                @block.sync
                def _(sy, dc=dc):
                    if dc >= 2:
                        wt(sy, f"{tag}_wsp{dc - 2}")
                    dmax(sy, "sLD", wstg[dc % 2][:],
                         wdram[128 * dc:128 * (dc + 1), :], label=f"{tag}_wld{dc}")

                @block.vector
                def _(vec, dc=dc):
                    wt(vec, f"{tag}_wld{dc}")
                    vec.tensor_copy(out=whb[:, 2048 * dc:2048 * (dc + 1)],
                                    in_=wstg[dc % 2][:])
                    mark(vec.scalar_tensor_tensor(
                        out=wlb[:, 2048 * dc:2048 * (dc + 1)], in0=wstg[dc % 2][:],
                        scalar=1.0, in1=whb[:, 2048 * dc:2048 * (dc + 1)],
                        op0=Alu.mult, op1=Alu.subtract),
                        "sDV", f"{tag}_wsp{dc}" if True else None)
            lab[f"{tag}_w"] = lab[f"{tag}_wsp15"]
            return whb, wlb

        def load_xg(tag, grp, src_scr, xgh, xgl, prev_lbl):
            @block.sync
            def _(sy):
                if prev_lbl:
                    wt(sy, prev_lbl)
                for hl, xg in ((0, xgh), (1, xgl)):
                    for dc in range(16):
                        dmax(sy, "sLD", xg[:, 512 * dc:512 * (dc + 1)],
                             src_scr[hl][dc][:, 512 * grp:512 * (grp + 1)],
                             label=f"{tag}_xg{grp}" if hl == 1 and dc == 15 else None)

        def mm3(te, psum, wh, wl, xh, xl, first, last):
            te.matmul(out=psum, lhsT=wh, rhs=xh, start=first, stop=False)
            te.matmul(out=psum, lhsT=wl, rhs=xh, start=False, stop=False)
            return te.matmul(out=psum, lhsT=wh, rhs=xl, start=False, stop=last)

        def p1c_ksweep(st):
            whb, wlb = load_w_split("k", Wk, st)
            xgh = sbT(st, "kxgh", [128, 8192], BF16)
            xgl = sbT(st, "kxgl", [128, 8192], BF16)
            kob = [[sbT(st, f"ko{i}{hl}", [128, 512], BF16) for hl in range(2)]
                   for i in range(2)]
            pp = [psT(st, "kp0", [128, 512]), psT(st, "kp1", [128, 512])]
            for grp in range(8):
                load_xg("k", grp, xhf_scr, xgh, xgl,
                        f"k_mm{grp - 1}_15" if grp > 0 else None)
                for h in range(16):
                    @block.tensor
                    def _(te, grp=grp, h=h):
                        wt(te, f"k_xg{grp}")
                        if grp == 0 and h == 0:
                            wt(te, "k_w")
                        prev = grp * 16 + h - 2
                        if prev >= 0:
                            wt(te, f"k_ev{prev}")
                        for dc in range(16):
                            wsl = slice(2048 * dc + 128 * h, 2048 * dc + 128 * (h + 1))
                            xsl = slice(512 * dc, 512 * (dc + 1))
                            _li = mm3(te, pp[h % 2][:], whb[:, wsl], wlb[:, wsl],
                                      xgh[:, xsl], xgl[:, xsl], dc == 0, dc == 15)
                        mark(_li, "sPE", f"k_mm{grp}_{h}")

                    @block.vector
                    def _(vec, grp=grp, h=h):
                        wt(vec, f"k_mm{grp}_{h}")
                        prev = grp * 16 + h - 2
                        if prev >= 0:
                            wt(vec, f"k_st{prev}")
                        vec.tensor_scalar(out=kob[h % 2][0][:], in0=pp[h % 2][:],
                                          scalar1=bkc[:, h:h + 1], scalar2=None,
                                          op0=Alu.add)
                        mark(vec.scalar_tensor_tensor(out=kob[h % 2][1][:],
                                                      in0=pp[h % 2][:],
                                                      scalar=bkc[:, h:h + 1],
                                                      in1=kob[h % 2][0][:],
                                                      op0=Alu.add, op1=Alu.subtract),
                             "sDV", f"k_ev{grp * 16 + h}")

                    @block.gpsimd
                    def _(gp, grp=grp, h=h):
                        wt(gp, f"k_ev{grp * 16 + h}")
                        dmax(gp, "sGP", k_scr[0][h][:, 512 * grp:512 * (grp + 1)],
                             kob[h % 2][0][:])
                        dmax(gp, "sGP", k_scr[1][h][:, 512 * grp:512 * (grp + 1)],
                             kob[h % 2][1][:], label=f"k_st{grp * 16 + h}")

        def p1d_vsweep(st):
            bv_b = sbT(st, "bv_b", [128, D])
            pbc_v = psT(st, "pbc_v", [128, 512])
            emit_bcast_row("bvb", bv_b, bv_row[:], pbc_v)
            whb, wlb = load_w_split("v", Wv, st)
            xgh = sbT(st, "vxgh", [128, 8192], BF16)
            xgl = sbT(st, "vxgl", [128, 8192], BF16)
            vf = sbT(st, "vf", [128, 512])
            vob = [[sbT(st, f"vo{i}{hl}", [128, 512], BF16) for hl in range(2)]
                   for i in range(2)]
            pp = [psT(st, "vp0", [128, 512]), psT(st, "vp1", [128, 512])]
            for grp in range(8):
                load_xg("v", grp, xhf_scr, xgh, xgl,
                        f"v_mm{grp - 1}_15" if grp > 0 else None)
                for sub in range(16):
                    tch, hdt = sub // 4, sub % 4

                    @block.tensor
                    def _(te, grp=grp, sub=sub, tch=tch, hdt=hdt):
                        wt(te, f"v_xg{grp}")
                        if grp == 0 and sub == 0:
                            wt(te, "v_w")
                            wt(te, "bvb_bc3")
                        prev = grp * 16 + sub - 2
                        if prev >= 0:
                            wt(te, f"v_ev{prev}")
                        for dc in range(16):
                            xsl = slice(512 * dc + 128 * tch, 512 * dc + 128 * (tch + 1))
                            wsl = slice(2048 * dc + 512 * hdt, 2048 * dc + 512 * (hdt + 1))
                            _li = mm3(te, pp[sub % 2][:], xgh[:, xsl], xgl[:, xsl],
                                      whb[:, wsl], wlb[:, wsl], dc == 0, dc == 15)
                        mark(_li, "sPE", f"v_mm{grp}_{sub}")

                    @block.vector
                    def _(vec, grp=grp, sub=sub, hdt=hdt):
                        wt(vec, f"v_mm{grp}_{sub}")
                        prev = grp * 16 + sub - 2
                        if prev >= 0:
                            wt(vec, f"v_st{prev}")
                        vec.tensor_tensor(out=vf[:], in0=pp[sub % 2][:],
                                          in1=bv_b[:, 512 * hdt:512 * (hdt + 1)],
                                          op=Alu.add)
                        vec.tensor_copy(out=vob[sub % 2][0][:], in_=vf[:])
                        mark(vec.tensor_tensor(out=vob[sub % 2][1][:], in0=vf[:],
                                               in1=vob[sub % 2][0][:], op=Alu.subtract),
                             "sDV", f"v_ev{grp * 16 + sub}")

                    @block.gpsimd
                    def _(gp, grp=grp, sub=sub, tch=tch, hdt=hdt):
                        wt(gp, f"v_ev{grp * 16 + sub}")
                        r0 = 512 * grp + 128 * tch
                        dmax(gp, "sGP", v_scr[0][r0:r0 + 128, 512 * hdt:512 * (hdt + 1)],
                             vob[sub % 2][0][:])
                        dmax(gp, "sGP", v_scr[1][r0:r0 + 128, 512 * hdt:512 * (hdt + 1)],
                             vob[sub % 2][1][:], label=f"v_st{grp * 16 + sub}")

        def p1e_qsweep(st):
            whb, wlb = load_w_split("q", Wq, st)
            xgh = sbT(st, "qxgh", [128, 8192], BF16)
            xgl = sbT(st, "qxgl", [128, 8192], BF16)
            qtmp = sbT(st, "qtmp", [128, 512])
            qtmp2 = sbT(st, "qtmp2", [128, 512])
            qv = sbT(st, "qv", [128, 512])
            qob = [[sbT(st, f"qo{i}{hl}", [128, 512], BF16) for hl in range(2)]
                   for i in range(2)]
            pp = [psT(st, "qp0", [128, 512]), psT(st, "qp1", [128, 512])]
            for grp in range(4):
                load_xg("q", grp, xhq_scr, xgh, xgl,
                        f"q_mm{grp - 1}_15" if grp > 0 else None)
                for h in range(16):
                    @block.tensor
                    def _(te, grp=grp, h=h):
                        wt(te, f"q_xg{grp}")
                        if grp == 0 and h == 0:
                            wt(te, "q_w")
                        prev = grp * 16 + h - 2
                        if prev >= 0:
                            wt(te, f"q_ev{prev}")
                        for dc in range(16):
                            wsl = slice(2048 * dc + 128 * h, 2048 * dc + 128 * (h + 1))
                            xsl = slice(512 * dc, 512 * (dc + 1))
                            _li = mm3(te, pp[h % 2][:], whb[:, wsl], wlb[:, wsl],
                                      xgh[:, xsl], xgl[:, xsl], dc == 0, dc == 15)
                        mark(_li, "sPE", f"q_mm{grp}_{h}")

                    @block.vector
                    def _(vec, grp=grp, h=h):
                        wt(vec, f"q_mm{grp}_{h}")
                        prev = grp * 16 + h - 2
                        if prev >= 0:
                            wt(vec, f"q_st{prev}")
                        vec.tensor_scalar(out=qtmp[:], in0=pp[h % 2][:],
                                          scalar1=bqc[:, h:h + 1], scalar2=None,
                                          op0=Alu.add)
                        vec.tensor_scalar(out=qtmp2[:], in0=qtmp[:],
                                          scalar1=C_HI, scalar2=None, op0=Alu.mult)
                        vec.scalar_tensor_tensor(out=qv[:], in0=qtmp[:],
                                                 scalar=C_LO, in1=qtmp2[:],
                                                 op0=Alu.mult, op1=Alu.add)
                        vec.tensor_copy(out=qob[h % 2][0][:], in_=qv[:])
                        mark(vec.tensor_tensor(out=qob[h % 2][1][:], in0=qv[:],
                                               in1=qob[h % 2][0][:], op=Alu.subtract),
                             "sDV", f"q_ev{grp * 16 + h}")

                    @block.gpsimd
                    def _(gp, grp=grp, h=h):
                        wt(gp, f"q_ev{grp * 16 + h}")
                        dmax(gp, "sGP", q_scr[0][h][:, 512 * grp:512 * (grp + 1)],
                             qob[h % 2][0][:])
                        dmax(gp, "sGP", q_scr[1][h][:, 512 * grp:512 * (grp + 1)],
                             qob[h % 2][1][:], label=f"q_st{grp * 16 + h}")

        if phases_upto >= 3:
            st = ExitStack(); p1c_ksweep(st); st.close()
            barrier()
            st = ExitStack(); p1d_vsweep(st); st.close()
            barrier()
            st = ExitStack(); p1e_qsweep(st); st.close()
            barrier()

        def p2(st):
            maskb = sbT(st, "maskb", [128, 1024])
            kb2 = [[sbT(st, f"kb{i}{hl}", [128, S], BF16) for hl in range(2)]
                   for i in range(2)]
            qb2 = [[sbT(st, f"qb{i}{hl}", [128, NQ], BF16) for hl in range(2)]
                   for i in range(2)]
            vb2 = [[sbT(st, f"vb{i}{hl}", [128, S], BF16) for hl in range(2)]
                   for i in range(2)]
            ab2 = [sbT(st, "ab0", [128, S]), sbT(st, "ab1", [128, S])]
            srow2 = [sbT(st, "srowA", [128, S]), sbT(st, "srowB", [128, S])]
            prow2 = [sbT(st, "prowA", [128, S]), sbT(st, "prowB", [128, S])]
            ptb = [[sbT(st, f"ptb{i}{hl}", [128, 512], BF16) for hl in range(2)]
                   for i in range(2)]
            ctb0 = [sbT(st, "ctb0A", [128, 256], BF16), sbT(st, "ctb0B", [128, 256], BF16)]
            ctb1 = [sbT(st, "ctb1A", [128, 256], BF16), sbT(st, "ctb1B", [128, 256], BF16)]
            psS = [psT(st, "psS0", [128, 512]), psT(st, "psS1", [128, 512])]
            pst = [psT(st, "psT0", [128, 512]), psT(st, "psT1", [128, 512])]
            psC = [psT(st, "psC0", [128, 128]), psT(st, "psC1", [128, 128])]
            psX = [psT(st, "psX0", [128, 256], BF16), psT(st, "psX1", [128, 256], BF16)]
            SB0 = 544

            def scol(name, stp):
                off = SB0 + {"rsum": 0, "nmax": 2, "rnorm": 4}[name] * 2 + stp
                return cols[:, off:off + 1]

            @block.sync
            def _(sy):
                dmax(sy, "sLD", maskb[:, 0:512], maskcd[0])
                dmax(sy, "sLD", maskb[:, 512:1024], maskcd[1], label="a_mask")

            def emit_loads(h):
                kb, qb, vb, ab = kb2[h % 2], qb2[h % 2], vb2[h % 2], ab2[h % 2]

                @block.sync
                def _(sy, h=h, kb=kb, qb=qb, vb=vb, ab=ab):
                    if h > 1:
                        wt(sy, f"a_ctxs{h - 2}_15")
                    for hl in range(2):
                        dmax(sy, "sLD", kb[hl][:], k_scr[hl][h][:, :])
                        dmax(sy, "sLD", qb[hl][:], q_scr[hl][h][:, :])
                        dmax(sy, "sLD", vb[hl][:].rearrange("p (kc c) -> p kc c", c=128),
                             v_scr[hl].rearrange("(kc p) f -> p kc f", p=128)[
                                 :, :, 128 * h:128 * (h + 1)])
                    dmax(sy, "sLD", ab[:], alibi[h], label=f"a_ld{h}")

            def emit_attn(h, i):
                stp = h % 2
                kb, qb, vb, ab = kb2[stp], qb2[stp], vb2[stp], ab2[stp]
                srow, prow = srow2[stp], prow2[stp]
                KT = i // 2 + 1
                L = 512 * KT
                c01 = i % 2
                qsl = slice(128 * i, 128 * (i + 1))
                ii = h * 16 + i
                ph, pi = (h, i - 1) if i > 0 else (h - 2, 15)
                for kt in range(KT):
                    @block.tensor
                    def _(te, h=h, i=i, kt=kt, kb=kb, qb=qb, qsl=qsl):
                        wt(te, f"a_ld{h}")
                        if h == 0 and i == 0 and kt == 0:
                            wt(te, "a_mask")
                        gid = _inc("a_smm", 1) - 1
                        if gid >= 2:
                            wt(te, f"a_sev{gid - 2}")
                        ksl = slice(512 * kt, 512 * (kt + 1))
                        _li = mm3(te, psS[gid % 2][:], qb[0][:, qsl], qb[1][:, qsl],
                                  kb[0][:, ksl], kb[1][:, ksl], True, True)
                        mark(_li, "sPE", f"a_smm{gid}")

                    @block.vector
                    def _(vec, h=h, i=i, kt=kt, KT=KT, c01=c01, ab=ab, srow=srow,
                          ph=ph, pi=pi):
                        gid = _inc("a_sev_c", 1) - 1
                        wt(vec, f"a_smm{gid}")
                        if kt == 0 and lab.get(f"a_exp{ph}_{pi}") is not None:
                            wt(vec, f"a_exp{ph}_{pi}")
                        mark(vec.tensor_tensor(out=srow[:, 512 * kt:512 * (kt + 1)],
                                               in0=psS[gid % 2][:],
                                               in1=ab[:, 512 * kt:512 * (kt + 1)],
                                               op=Alu.add),
                             "sDV", f"a_sev{gid}")
                        if kt == KT - 1:
                            mark(vec.tensor_tensor(
                                out=srow[:, 512 * kt:512 * (kt + 1)],
                                in0=srow[:, 512 * kt:512 * (kt + 1)],
                                in1=maskb[:, 512 * c01:512 * (c01 + 1)],
                                op=Alu.add), "sDV", f"a_msk{h}_{i}")

                @block.vector
                def _(vec, h=h, i=i, L=L, srow=srow, stp=stp):
                    wt(vec, f"a_msk{h}_{i}")
                    mark(vec.reduce_max(out=scol("nmax", stp), in_=srow[:, :L],
                                        axis=AX, negate=True),
                         "sDV", f"a_nm{h}_{i}")

                @block.scalar
                def _(sc, h=h, i=i, L=L, srow=srow, prow=prow, stp=stp, ph=ph, pi=pi):
                    wt(sc, f"a_nm{h}_{i}")
                    if lab.get(f"a_avmm{ph}_{pi}_last") is not None:
                        wt(sc, f"a_avmm{ph}_{pi}_last")
                    mark(sc.activation(out=prow[:, :L], in_=srow[:, :L], func=AF.Exp,
                                       bias=scol("nmax", stp), scale=1.0,
                                       accum_out=scol("rsum", stp)),
                         "sAC", f"a_exp{h}_{i}")

                @block.vector
                def _(vec, h=h, i=i, stp=stp):
                    wt(vec, f"a_exp{h}_{i}")
                    mark(vec.reciprocal(out=scol("rnorm", stp), in_=scol("rsum", stp)),
                         "sDV", f"a_rn{h}_{i}")

                for qk in range(KT):
                    @block.tensor
                    def _(te, h=h, i=i, qk=qk, prow=prow):
                        wt(te, f"a_exp{h}_{i}")
                        gid = _inc("a_ptq", 1) - 1
                        if gid >= 2:
                            wt(te, f"a_pte{gid - 2}")
                        for u in range(4):
                            kc = 4 * qk + u
                            _li = te.transpose(out=pst[gid % 2][:, 128 * u:128 * (u + 1)],
                                               in_=prow[:, 128 * kc:128 * (kc + 1)],
                                               identity=ident[:])
                        mark(_li, "sPE", f"a_pt{gid}")

                    @block.vector
                    def _(vec, h=h, i=i, qk=qk):
                        gid = _inc("a_pte_c", 1) - 1
                        wt(vec, f"a_pt{gid}")
                        if gid >= 2:
                            wt(vec, f"a_avg{gid - 2}")
                        vec.tensor_copy(out=ptb[gid % 2][0][:], in_=pst[gid % 2][:])
                        mark(vec.scalar_tensor_tensor(out=ptb[gid % 2][1][:],
                                                      in0=pst[gid % 2][:],
                                                      scalar=1.0,
                                                      in1=ptb[gid % 2][0][:],
                                                      op0=Alu.mult,
                                                      op1=Alu.subtract),
                             "sDV", f"a_pte{gid}")

                    @block.tensor
                    def _(te, h=h, i=i, qk=qk, KT=KT, vb=vb, ii=ii):
                        gid = _inc("a_avq", 1) - 1
                        wt(te, f"a_pte{gid}")
                        if qk == 0:
                            zh, zi = (h, i - 2) if i > 1 else (h - 2, 14 + i)
                            if lab.get(f"a_ctxn{zh}_{zi}") is not None:
                                wt(te, f"a_ctxn{zh}_{zi}")
                        for u in range(4):
                            kc = 4 * qk + u
                            vsl = slice(128 * kc, 128 * (kc + 1))
                            psl = slice(128 * u, 128 * (u + 1))
                            te.matmul(out=psC[ii % 2][:], lhsT=ptb[gid % 2][0][:, psl],
                                      rhs=vb[0][:, vsl],
                                      start=(qk == 0 and u == 0), stop=False)
                            te.matmul(out=psC[ii % 2][:], lhsT=ptb[gid % 2][1][:, psl],
                                      rhs=vb[0][:, vsl], start=False, stop=False)
                            _li = te.matmul(out=psC[ii % 2][:], lhsT=ptb[gid % 2][0][:, psl],
                                            rhs=vb[1][:, vsl], start=False,
                                            stop=(qk == KT - 1 and u == 3))
                        lbl = f"a_avmm{h}_{i}_last" if qk == KT - 1 else f"a_avg_m{gid}"
                        mark(_li, "sPE", lbl)
                        lab[f"a_avg{gid}"] = ("sPE", cnt["sPE"])

                @block.vector
                def _(vec, h=h, i=i, stp=stp, ii=ii, ph=ph, pi=pi):
                    wt(vec, f"a_avmm{h}_{i}_last")
                    wt(vec, f"a_rn{h}_{i}")
                    if lab.get(f"a_ctT{ph}_{pi}") is not None:
                        wt(vec, f"a_ctT{ph}_{pi}")
                    vec.tensor_scalar(out=ctb0[stp][:, 0:128], in0=psC[ii % 2][:],
                                      scalar1=scol("rnorm", stp), scalar2=None,
                                      op0=Alu.mult)
                    mark(vec.scalar_tensor_tensor(out=ctb0[stp][:, 128:256],
                                                  in0=psC[ii % 2][:],
                                                  scalar=scol("rnorm", stp),
                                                  in1=ctb0[stp][:, 0:128],
                                                  op0=Alu.mult, op1=Alu.subtract),
                         "sDV", f"a_ctxn{h}_{i}")

                @block.tensor
                def _(te, h=h, i=i, stp=stp, ph=ph, pi=pi):
                    wt(te, f"a_ctxn{h}_{i}")
                    if lab.get(f"a_ctxe{ph}_{pi}") is not None:
                        wt(te, f"a_ctxe{ph}_{pi}")
                    te.transpose(out=psX[stp][:, 0:128], in_=ctb0[stp][:, 0:128],
                                 identity=identb[:])
                    mark(te.transpose(out=psX[stp][:, 128:256], in_=ctb0[stp][:, 128:256],
                                      identity=identb[:]),
                         "sPE", f"a_ctT{h}_{i}")

                @block.vector
                def _(vec, h=h, i=i, stp=stp, ph=ph, pi=pi):
                    wt(vec, f"a_ctT{h}_{i}")
                    if lab.get(f"a_ctxs{ph}_{pi}") is not None:
                        wt(vec, f"a_ctxs{ph}_{pi}")
                    mark(vec.tensor_copy(out=ctb1[stp][:], in_=psX[stp][:]),
                         "sDV", f"a_ctxe{h}_{i}")

                @block.gpsimd
                def _(gp, h=h, i=i, stp=stp):
                    wt(gp, f"a_ctxe{h}_{i}")
                    dmax(gp, "sGP", ctxT_scr[0][i][128 * h:128 * (h + 1), :],
                         ctb1[stp][:, 0:128])
                    dmax(gp, "sGP", ctxT_scr[1][i][128 * h:128 * (h + 1), :],
                         ctb1[stp][:, 128:256], label=f"a_ctxs{h}_{i}")

            for hp in range(8):
                emit_loads(2 * hp)
                emit_loads(2 * hp + 1)
                for i in range(16):
                    emit_attn(2 * hp, i)
                    emit_attn(2 * hp + 1, i)

        if phases_upto >= 4:
            st = ExitStack(); p2(st); st.close()
            barrier()

        def p3(st):
            bo_b = sbT(st, "bo_b", [128, D])
            pbc_o = psT(st, "pbc_o", [128, 512])
            emit_bcast_row("bob", bo_b, bo_row[:], pbc_o)
            woh, wol = load_w_split("o", Wo, st)
            ctq2 = [[sbT(st, f"ctq{i}{hl}", [128, 2048], BF16) for hl in range(2)]
                    for i in range(2)]
            xqt = sbT(st, "xqt", [128, D])
            xbo = sbT(st, "xbo", [128, D])
            x2t = sbT(st, "x2t", [128, D])
            psO = [psT(st, "psO0", [128, 512]), psT(st, "psO1", [128, 512])]
            for qc in range(16):
                ctq = ctq2[qc % 2]

                @block.sync
                def _(sy, qc=qc, ctq=ctq):
                    if qc > 1:
                        wt(sy, f"o_mm{qc - 2}_3")
                    if qc > 0:
                        wt(sy, f"o_x2s{qc - 1}")
                    for hl in range(2):
                        dmax(sy, "sLD",
                             ctq[hl][:].rearrange("p (fc q) -> p fc q", q=128),
                             ctxT_scr[hl][qc].rearrange("(fc p) q -> p fc q", p=128))
                    dmax(sy, "sLD", xqt[:], x_q[128 * qc:128 * (qc + 1), :],
                         label=f"o_ld{qc}")

                @block.vector
                def _(vec, qc=qc):
                    wt(vec, f"o_ld{qc}")
                    if qc == 0:
                        wt(vec, "bob_bc3")
                    mark(vec.tensor_tensor(out=xbo[:], in0=xqt[:], in1=bo_b[:],
                                           op=Alu.add), "sDV", f"o_xbo{qc}")

                for dt in range(4):
                    @block.tensor
                    def _(te, qc=qc, dt=dt):
                        wt(te, f"o_ld{qc}")
                        if qc == 0 and dt == 0:
                            wt(te, "o_w")
                        gid = qc * 4 + dt
                        if gid >= 2:
                            wt(te, f"o_ev{gid - 2}")
                        ctq = ctq2[qc % 2]
                        for fc in range(16):
                            csl = slice(128 * fc, 128 * (fc + 1))
                            wsl = slice(2048 * fc + 512 * dt, 2048 * fc + 512 * (dt + 1))
                            _li = mm3(te, psO[gid % 2][:], ctq[0][:, csl], ctq[1][:, csl],
                                      woh[:, wsl], wol[:, wsl], fc == 0, fc == 15)
                        mark(_li, "sPE", f"o_mm{qc}_{dt}")

                    @block.vector
                    def _(vec, qc=qc, dt=dt):
                        gid = qc * 4 + dt
                        wt(vec, f"o_mm{qc}_{dt}")
                        wt(vec, f"o_xbo{qc}")
                        mark(vec.tensor_tensor(out=x2t[:, 512 * dt:512 * (dt + 1)],
                                               in0=psO[gid % 2][:],
                                               in1=xbo[:, 512 * dt:512 * (dt + 1)],
                                               op=Alu.add),
                             "sDV", f"o_ev{gid}")

                @block.scalar
                def _(sc, qc=qc):
                    wt(sc, f"o_ev{qc * 4 + 3}")
                    mark(sc.activation(out=xqt[:], in_=x2t[:], func=AF.Square,
                                       accum_out=col("sq2", 1, qc)),
                         "sAC", f"o_sq{qc}")

                @block.vector
                def _(vec, qc=qc):
                    wt(vec, f"o_ev{qc * 4 + 3}")
                    mark(vec.reduce_sum(out=col("sum2", 1, qc), in_=x2t[:], axis=AX),
                         "sDV", f"o_sum{qc}")

                @block.gpsimd
                def _(gp, qc=qc):
                    wt(gp, f"o_sq{qc}")
                    wt(gp, f"o_sum{qc}")
                    dmax(gp, "sGP", x2_scr[128 * qc:128 * (qc + 1), :], x2t[:],
                         label=f"o_x2s{qc}")

        if phases_upto >= 5:
            st = ExitStack(); p3(st); st.close()
            p1a_colmath("l2", 16, "sum2", "sq2", "mean2", "var2", "rstd2",
                        1.0 / 2048, "o_sq15", "o_sum15")
            barrier()

        st4 = ExitStack()
        if phases_upto >= 6:
            xqT = sbT(st4, "xqT", [128, 32768], BF16)
            sta = ExitStack()
            xa = [sbT(sta, "f_xa0", [128, D]), sbT(sta, "f_xa1", [128, D])]
            xh2 = sbT(sta, "f_xh2", [128, D])
            ub = sbT(sta, "f_u", [128, D])
            vvb = sbT(sta, "f_vv", [128, D])
            bb = sbT(sta, "f_bb", [128, D])
            xqbf = sbT(sta, "f_xqbf", [128, D], BF16)
            psq = [psT(sta, "f_ps0", [128, 512], BF16), psT(sta, "f_ps1", [128, 512], BF16)]
            for tc in range(16):
                @block.sync
                def _(sy, tc=tc):
                    if tc > 1:
                        wt(sy, f"f_u{tc - 2}")
                    dmax(sy, "sLD", xa[tc % 2][:], x2_scr[128 * tc:128 * (tc + 1), :],
                         label=f"f_x2{tc}")

                @block.vector
                def _(vec, tc=tc):
                    wt(vec, f"f_x2{tc}")
                    if tc == 0:
                        wt(vec, "l2_rstd")
                    if tc > 0:
                        wt(vec, f"f_xqe{tc - 1}_3")
                        wt(vec, f"f_xqt{tc - 1}_3")
                    vec.tensor_scalar(out=xh2[:], in0=xa[tc % 2][:],
                                      scalar1=col("mean2", 1, tc),
                                      scalar2=col("rstd2", 1, tc),
                                      op0=Alu.subtract, op1=Alu.mult)
                    tick(vec, vec.reduce_max(out=col("tm"), in_=xh2[:], axis=AX,
                                             apply_absolute_value=True))
                    tick(vec, vec.tensor_scalar(out=col("tmpa"), in0=col("tm"),
                                                scalar1=1e-5, scalar2=None, op0=Alu.max))
                    tick(vec, vec.reciprocal(out=col("tmpb"), in_=col("tmpa")))
                    tick(vec, vec.tensor_scalar(out=col("sclx", 1, tc), in0=col("tmpb"),
                                                scalar1=127.0, scalar2=None, op0=Alu.mult))
                    tick(vec, vec.reciprocal(out=col("tmpc"), in_=col("sclx", 1, tc)))
                    tick(vec, vec.tensor_scalar(out=col("rsx1n", 1, tc), in0=col("tmpc"),
                                                scalar1=-1.0, scalar2=None, op0=Alu.mult))
                    vec.tensor_scalar(out=ub[:], in0=xh2[:],
                                      scalar1=col("sclx", 1, tc), scalar2=128.5,
                                      op0=Alu.mult, op1=Alu.add)
                    vec.tensor_copy(out=vvb[:], in_=ub[:])
                    for j in range(7, -1, -1):
                        vec.tensor_scalar(out=bb[:], in0=vvb[:], scalar1=float(2 ** j),
                                          scalar2=float(2 ** j), op0=Alu.is_ge,
                                          op1=Alu.mult)
                        vec.tensor_tensor(out=vvb[:], in0=vvb[:], in1=bb[:],
                                          op=Alu.subtract)
                    vec.tensor_tensor(out=bb[:], in0=ub[:], in1=vvb[:], op=Alu.subtract)
                    mark(vec.tensor_scalar(out=xqbf[:], in0=bb[:], scalar1=-128.0,
                                           scalar2=None, op0=Alu.add),
                         "sDV", f"f_u{tc}")

                for qd in range(4):
                    @block.tensor
                    def _(te, tc=tc, qd=qd):
                        wt(te, f"f_u{tc}")
                        gid = tc * 4 + qd
                        if gid >= 2:
                            wt(te, f"f_xqe{(gid - 2) // 4}_{(gid - 2) % 4}")
                        for u in range(4):
                            dc = qd * 4 + u
                            _li = te.transpose(out=psq[qd % 2][:, 128 * u:128 * (u + 1)],
                                               in_=xqbf[:, 128 * dc:128 * (dc + 1)],
                                               identity=identb[:])
                        mark(_li, "sPE", f"f_xqt{tc}_{qd}")

                    @block.vector
                    def _(vec, tc=tc, qd=qd):
                        wt(vec, f"f_xqt{tc}_{qd}")
                        dst = xqT[:, 2048 * tc + 512 * qd:2048 * tc + 512 * (qd + 1)]
                        mark(vec.tensor_copy(out=dst, in_=psq[qd % 2][:]),
                             "sDV", f"f_xqe{tc}_{qd}")
            sta.close()
            barrier()

        if phases_upto >= 7:
            stb = ExitStack()
            wa = [sbT(stb, "g_wa", [128, D]), sbT(stb, "g_wb", [128, D])]
            b1 = sbT(stb, "g_b1", [128, D])
            tnn = sbT(stb, "g_tnn", [128, D], BF16)
            slab = sbT(stb, "g_slab", [128, 8192], BF16)
            swrow = sbT(stb, "g_swrow", [1, 512])
            sw1b = sbT(stb, "g_sw1b", [128, 512])
            hpre = sbT(stb, "g_hpre", [128, 512])
            hsq = sbT(stb, "g_hsq", [128, 512])
            x3b = sbT(stb, "g_x3", [128, 512])
            thb = sbT(stb, "g_th", [128, 512])
            v1g = [sbT(stb, "g_v1a", [128, 512]), sbT(stb, "g_v1b", [128, 512])]
            tmp5 = sbT(stb, "g_tmp5", [128, 512])
            psH = [psT(stb, "g_psH0", [128, 512]), psT(stb, "g_psH1", [128, 512])]
            psW = [psT(stb, "g_psW0", [128, 512], BF16),
                   psT(stb, "g_psW1", [128, 512], BF16)]
            psR = psT(stb, "g_psR", [128, 512])
            psB = psT(stb, "g_psB", [128, 512])

            @block.vector
            def _(vec):
                mark(vec.memset(col("ghmax", 16), 0.0), "sDV", "g_ghinit")

            for ht in range(16):
                for c in range(4):
                    @block.sync
                    def _(sy, ht=ht, c=c):
                        gid = ht * 4 + c
                        if gid >= 2:
                            wt(sy, f"g_tnn{gid - 2}")
                        r0 = 512 * ht + 128 * c
                        dmax(sy, "sLD", wa[gid % 2][:], w1[r0:r0 + 128, :],
                             label=f"g_w{gid}")

                    @block.vector
                    def _(vec, ht=ht, c=c):
                        gid = ht * 4 + c
                        wt(vec, f"g_w{gid}")
                        tick(vec, vec.reduce_sum(out=col("tm"), in_=wa[gid % 2][:],
                                                 axis=AX, apply_absolute_value=True))
                        tick(vec, vec.tensor_scalar(out=col("tmpa"), in0=col("tm"),
                                                    scalar1=1.0 / 2048, scalar2=None,
                                                    op0=Alu.mult))
                        tick(vec, vec.tensor_scalar(out=col("tmpb"), in0=col("tmpa"),
                                                    scalar1=1e-5, scalar2=None,
                                                    op0=Alu.max))
                        tick(vec, vec.tensor_scalar(out=col("swq", 1, c), in0=col("tmpb"),
                                                    scalar1=0.5, scalar2=None,
                                                    op0=Alu.mult))
                        tick(vec, vec.tensor_scalar(out=col("tmpd"), in0=col("swq", 1, c),
                                                    scalar1=-1.0, scalar2=None,
                                                    op0=Alu.mult))
                        vec.tensor_scalar(out=b1[:], in0=wa[gid % 2][:],
                                          scalar1=col("swq", 1, c), scalar2=None,
                                          op0=Alu.is_gt)
                        mark(vec.scalar_tensor_tensor(out=tnn[:], in0=wa[gid % 2][:],
                                                      scalar=col("tmpd"), in1=b1[:],
                                                      op0=Alu.is_lt, op1=Alu.subtract),
                             "sDV", f"g_tnn{gid}")

                    for qd in range(4):
                        @block.tensor
                        def _(te, ht=ht, c=c, qd=qd):
                            gid = (ht * 4 + c) * 4 + qd
                            wt(te, f"g_tnn{ht * 4 + c}")
                            if gid >= 2:
                                wt(te, f"g_sle{gid - 2}")
                            for u in range(4):
                                dc = qd * 4 + u
                                _li = te.transpose(out=psW[gid % 2][:, 128 * u:128 * (u + 1)],
                                                   in_=tnn[:, 128 * dc:128 * (dc + 1)],
                                                   identity=identb[:])
                            mark(_li, "sPE", f"g_slt{gid}")

                        @block.vector
                        def _(vec, ht=ht, c=c, qd=qd):
                            gid = (ht * 4 + c) * 4 + qd
                            wt(vec, f"g_slt{gid}")
                            dst = slab[:].rearrange(
                                "p (dc c4 cc) -> p dc c4 cc", dc=16, c4=4)[
                                :, 4 * qd:4 * qd + 4, c, :]
                            mark(vec.tensor_copy(
                                out=dst,
                                in_=psW[gid % 2][:].rearrange("p (u cc) -> p u cc", u=4)),
                                "sDV", f"g_sle{gid}")

                    @block.tensor
                    def _(te, ht=ht, c=c):
                        wt(te, f"g_tnn{ht * 4 + c}")
                        if lab.get(f"g_swe{ht}_{c - 1}") is not None:
                            wt(te, f"g_swe{ht}_{c - 1}")
                        mark(te.transpose(out=psR[0:1, 128 * c:128 * (c + 1)],
                                          in_=col("swq", 1, c), identity=ident[:]),
                             "sPE", f"g_swt{ht}_{c}")

                    @block.vector
                    def _(vec, ht=ht, c=c):
                        wt(vec, f"g_swt{ht}_{c}")
                        mark(vec.tensor_copy(out=swrow[0:1, 128 * c:128 * (c + 1)],
                                             in_=psR[0:1, 128 * c:128 * (c + 1)]),
                             "sDV", f"g_swe{ht}_{c}")

                @block.tensor
                def _(te, ht=ht):
                    wt(te, f"g_swe{ht}_3")
                    if lab.get(f"g_swbe{ht - 1}") is not None:
                        wt(te, f"g_swbe{ht - 1}")
                    mark(te.matmul(out=psB[:], lhsT=ones1[:], rhs=swrow[0:1, :],
                                   start=True, stop=True), "sPE", f"g_swb{ht}")

                @block.vector
                def _(vec, ht=ht):
                    wt(vec, f"g_swb{ht}")
                    mark(vec.tensor_scalar(out=sw1b[:], in0=psB[:], scalar1=2.0,
                                           scalar2=None, op0=Alu.mult),
                         "sDV", f"g_swbe{ht}")

                for tc in range(16):
                    @block.tensor
                    def _(te, ht=ht, tc=tc):
                        gid = ht * 16 + tc
                        wt(te, f"g_sle{(ht * 4 + 3) * 4 + 3}")
                        wt(te, f"f_xqe{tc}_3")
                        if gid >= 2:
                            wt(te, f"g_hev{gid - 2}")
                        for dc in range(16):
                            _li = te.matmul(out=psH[gid % 2][:],
                                            lhsT=xqT[:, 2048 * tc + 128 * dc:2048 * tc + 128 * (dc + 1)],
                                            rhs=slab[:, 512 * dc:512 * (dc + 1)],
                                            start=(dc == 0), stop=(dc == 15))
                        mark(_li, "sPE", f"g_hmm{gid}")

                    @block.vector
                    def _(vec, ht=ht, tc=tc):
                        gid = ht * 16 + tc
                        wt(vec, f"g_hmm{gid}")
                        wt(vec, f"g_swbe{ht}")
                        if gid > 0:
                            wt(vec, f"g_x3{gid - 1}")
                        mark(vec.scalar_tensor_tensor(out=hpre[:], in0=psH[gid % 2][:],
                                                      scalar=col("rsx1n", 1, tc),
                                                      in1=sw1b[:], op0=Alu.mult,
                                                      op1=Alu.mult),
                             "sDV", f"g_hev{gid}")

                    @block.vector
                    def _(vec, ht=ht, tc=tc):
                        gid = ht * 16 + tc
                        wt(vec, f"g_hev{gid}")
                        mark(vec.tensor_tensor(out=hsq[:], in0=hpre[:], in1=hpre[:],
                                               op=Alu.mult), "sDV", f"g_hsq{gid}")

                    @block.vector
                    def _(vec, ht=ht, tc=tc):
                        gid = ht * 16 + tc
                        wt(vec, f"g_hsq{gid}")
                        vec.tensor_scalar(out=tmp5[:], in0=hsq[:], scalar1=C1_HI,
                                          scalar2=GC_HI, op0=Alu.mult, op1=Alu.add)
                        vec.tensor_scalar(out=x3b[:], in0=hsq[:], scalar1=C1_LO,
                                          scalar2=GC_LO, op0=Alu.mult, op1=Alu.add)
                        vec.tensor_tensor(out=tmp5[:], in0=tmp5[:], in1=x3b[:],
                                          op=Alu.add)
                        mark(vec.tensor_tensor(out=x3b[:], in0=tmp5[:], in1=hpre[:],
                                               op=Alu.mult),
                             "sDV", f"g_x3{gid}")

                    @block.scalar
                    def _(sc, ht=ht, tc=tc):
                        gid = ht * 16 + tc
                        wt(sc, f"g_x3{gid}")
                        mark(sc.activation(out=thb[:], in_=x3b[:], func=AF.Tanh),
                             "sAC", f"g_th{gid}")

                    @block.vector
                    def _(vec, ht=ht, tc=tc):
                        gid = ht * 16 + tc
                        wt(vec, f"g_th{gid}")
                        if gid >= 2:
                            wt(vec, f"g_v1s{gid - 2}")
                        mark(vec.scalar_tensor_tensor(out=v1g[gid % 2][:], in0=thb[:],
                                                      scalar=1.0, in1=hpre[:],
                                                      op0=Alu.add, op1=Alu.mult),
                             "sDV", f"g_v1{gid}")
                        tick(vec, vec.reduce_max(out=col("tm"), in_=v1g[gid % 2][:],
                                                 axis=AX, apply_absolute_value=True))
                        mark(vec.tensor_tensor(out=col("ghmax", 1, tc),
                                               in0=col("ghmax", 1, tc), in1=col("tm"),
                                               op=Alu.max),
                             "sDV", f"g_gm{gid}")

                    @block.gpsimd
                    def _(gp, ht=ht, tc=tc):
                        gid = ht * 16 + tc
                        wt(gp, f"g_gm{gid}")
                        dmax(gp, "sGP", v1_scr[tc][:, 512 * ht:512 * (ht + 1)],
                             v1g[gid % 2][:], label=f"g_v1s{gid}")

            @block.vector
            def _(vec):
                wt(vec, f"g_gm{16 * 16 - 1}")
                tick(vec, vec.tensor_scalar(out=col("tmpa", 16), in0=col("ghmax", 16),
                                            scalar1=0.5, scalar2=None, op0=Alu.mult))
                tick(vec, vec.tensor_scalar(out=col("tmpb", 16), in0=col("tmpa", 16),
                                            scalar1=1e-5, scalar2=None, op0=Alu.max))
                tick(vec, vec.reciprocal(out=col("tmpc", 16), in_=col("tmpb", 16)))
                tick(vec, vec.tensor_scalar(out=col("tmpd", 16), in0=col("tmpc", 16),
                                            scalar1=127.0, scalar2=None, op0=Alu.mult))
                tick(vec, vec.tensor_scalar(out=col("s05", 16), in0=col("tmpd", 16),
                                            scalar1=0.5, scalar2=None, op0=Alu.mult))
                tick(vec, vec.reciprocal(out=col("tmpa", 16), in_=col("tmpd", 16)))
                mark(vec.tensor_scalar(out=col("rs2neg", 16), in0=col("tmpa", 16),
                                       scalar1=-1.0, scalar2=None, op0=Alu.mult),
                     "sDV", "g_scl2")
            stb.close()
            barrier()
            st4.close()

        if phases_upto >= 8:
            stc = ExitStack()
            va = [sbT(stc, "h_va0", [128, HID]), sbT(stc, "h_va1", [128, HID])]
            ub = sbT(stc, "h_u", [128, HID])
            vvb = sbT(stc, "h_vv", [128, HID])
            bb = sbT(stc, "h_bb", [128, HID])
            nqb = sbT(stc, "h_nq", [128, HID], BF16)
            hqe = sbT(stc, "h_hqe", [128, 8192], BF16)
            psq = [psT(stc, "h_ps0", [128, 512], BF16), psT(stc, "h_ps1", [128, 512], BF16)]
            for tc in range(16):
                @block.sync
                def _(sy, tc=tc):
                    if tc > 1:
                        wt(sy, f"h_nq{tc - 2}")
                    dmax(sy, "sLD", va[tc % 2][:], v1_scr[tc][:, :], label=f"h_v1{tc}")

                @block.vector
                def _(vec, tc=tc):
                    wt(vec, f"h_v1{tc}")
                    if tc == 0:
                        wt(vec, "g_scl2")
                    if tc > 0:
                        wt(vec, f"h_hqs{tc - 1}")
                    vec.tensor_scalar(out=ub[:], in0=va[tc % 2][:],
                                      scalar1=col("s05", 1, tc), scalar2=128.5,
                                      op0=Alu.mult, op1=Alu.add)
                    vec.tensor_copy(out=vvb[:], in_=ub[:])
                    for j in range(7, -1, -1):
                        vec.tensor_scalar(out=bb[:], in0=vvb[:], scalar1=float(2 ** j),
                                          scalar2=float(2 ** j), op0=Alu.is_ge,
                                          op1=Alu.mult)
                        vec.tensor_tensor(out=vvb[:], in0=vvb[:], in1=bb[:],
                                          op=Alu.subtract)
                    vec.tensor_tensor(out=bb[:], in0=ub[:], in1=vvb[:], op=Alu.subtract)
                    mark(vec.tensor_scalar(out=nqb[:], in0=bb[:], scalar1=-128.0,
                                           scalar2=None, op0=Alu.add),
                         "sDV", f"h_nq{tc}")

                for qd in range(16):
                    @block.tensor
                    def _(te, tc=tc, qd=qd):
                        gid = tc * 16 + qd
                        wt(te, f"h_nq{tc}")
                        if gid >= 2:
                            wt(te, f"h_te{gid - 2}")
                        for u in range(4):
                            hc = qd * 4 + u
                            _li = te.transpose(out=psq[qd % 2][:, 128 * u:128 * (u + 1)],
                                               in_=nqb[:, 128 * hc:128 * (hc + 1)],
                                               identity=identb[:])
                        mark(_li, "sPE", f"h_tt{gid}")

                    @block.vector
                    def _(vec, tc=tc, qd=qd):
                        gid = tc * 16 + qd
                        wt(vec, f"h_tt{gid}")
                        mark(vec.tensor_copy(out=hqe[:, 512 * qd:512 * (qd + 1)],
                                             in_=psq[qd % 2][:]),
                             "sDV", f"h_te{gid}")

                @block.gpsimd
                def _(gp, tc=tc):
                    wt(gp, f"h_te{tc * 16 + 15}")
                    dmax(gp, "sGP",
                         hqT_scr[tc].rearrange("hc p t -> p hc t"),
                         hqe[:].rearrange("p (hc t) -> p hc t", t=128),
                         label=f"h_hqs{tc}")
            stc.close()
            barrier()

        if phases_upto >= 9:
            std = ExitStack()
            w2a = sbT(std, "m_w2a", [128, HID])
            b1 = sbT(std, "m_b1", [128, HID])
            tnn = sbT(std, "m_tnn", [128, HID], BF16)
            slab2 = sbT(std, "m_slab2", [128, 32768], BF16)
            hqt = [sbT(std, "m_hqt0", [128, 8192], BF16),
                   sbT(std, "m_hqt1", [128, 8192], BF16)]
            xsl = [sbT(std, "m_xsl0", [128, 512]), sbT(std, "m_xsl1", [128, 512])]
            otb = [sbT(std, "m_ot0", [128, 512]), sbT(std, "m_ot1", [128, 512])]
            e1b = sbT(std, "m_e1", [128, 512])
            swrow = sbT(std, "m_swrow", [1, 512])
            sw2b = sbT(std, "m_sw2b", [128, 512])
            psF = [psT(std, "m_psF0", [128, 512]), psT(std, "m_psF1", [128, 512])]
            psW = [psT(std, "m_psW0", [128, 512], BF16),
                   psT(std, "m_psW1", [128, 512], BF16)]
            psR = psT(std, "m_psR", [128, 512])
            psB = psT(std, "m_psB", [128, 512])
            for dt in range(4):
                for c in range(4):
                    @block.sync
                    def _(sy, dt=dt, c=c):
                        gid = dt * 4 + c
                        if gid > 0:
                            wt(sy, f"m_tnn{gid - 1}")
                        r0 = 512 * dt + 128 * c
                        dmax(sy, "sLD", w2a[:], w2[r0:r0 + 128, :],
                             label=f"m_w{gid}")

                    @block.vector
                    def _(vec, dt=dt, c=c):
                        gid = dt * 4 + c
                        wt(vec, f"m_w{gid}")
                        tick(vec, vec.reduce_sum(out=col("tm"), in_=w2a[:], axis=AX,
                                                 apply_absolute_value=True))
                        tick(vec, vec.tensor_scalar(out=col("tmpa"), in0=col("tm"),
                                                    scalar1=1.0 / 8192, scalar2=None,
                                                    op0=Alu.mult))
                        tick(vec, vec.tensor_scalar(out=col("tmpb"), in0=col("tmpa"),
                                                    scalar1=1e-5, scalar2=None,
                                                    op0=Alu.max))
                        tick(vec, vec.tensor_scalar(out=col("swq", 1, c), in0=col("tmpb"),
                                                    scalar1=0.5, scalar2=None,
                                                    op0=Alu.mult))
                        tick(vec, vec.tensor_scalar(out=col("tmpd"), in0=col("swq", 1, c),
                                                    scalar1=-1.0, scalar2=None,
                                                    op0=Alu.mult))
                        vec.tensor_scalar(out=b1[:], in0=w2a[:],
                                          scalar1=col("swq", 1, c), scalar2=None,
                                          op0=Alu.is_gt)
                        mark(vec.scalar_tensor_tensor(out=tnn[:], in0=w2a[:],
                                                      scalar=col("tmpd"), in1=b1[:],
                                                      op0=Alu.is_lt, op1=Alu.subtract),
                             "sDV", f"m_tnn{gid}")

                    for qd in range(16):
                        @block.tensor
                        def _(te, dt=dt, c=c, qd=qd):
                            gid = (dt * 4 + c) * 16 + qd
                            wt(te, f"m_tnn{dt * 4 + c}")
                            if gid >= 2:
                                wt(te, f"m_sle{gid - 2}")
                            for u in range(4):
                                hc = qd * 4 + u
                                _li = te.transpose(out=psW[gid % 2][:, 128 * u:128 * (u + 1)],
                                                   in_=tnn[:, 128 * hc:128 * (hc + 1)],
                                                   identity=identb[:])
                            mark(_li, "sPE", f"m_slt{gid}")

                        @block.vector
                        def _(vec, dt=dt, c=c, qd=qd):
                            gid = (dt * 4 + c) * 16 + qd
                            wt(vec, f"m_slt{gid}")
                            dst = slab2[:].rearrange(
                                "p (hc c4 cc) -> p hc c4 cc", hc=64, c4=4)[
                                :, 4 * qd:4 * qd + 4, c, :]
                            mark(vec.tensor_copy(
                                out=dst,
                                in_=psW[gid % 2][:].rearrange("p (u cc) -> p u cc", u=4)),
                                "sDV", f"m_sle{gid}")

                    @block.tensor
                    def _(te, dt=dt, c=c):
                        wt(te, f"m_tnn{dt * 4 + c}")
                        if lab.get(f"m_swe{dt}_{c - 1}") is not None:
                            wt(te, f"m_swe{dt}_{c - 1}")
                        mark(te.transpose(out=psR[0:1, 128 * c:128 * (c + 1)],
                                          in_=col("swq", 1, c), identity=ident[:]),
                             "sPE", f"m_swt{dt}_{c}")

                    @block.vector
                    def _(vec, dt=dt, c=c):
                        wt(vec, f"m_swt{dt}_{c}")
                        mark(vec.tensor_copy(out=swrow[0:1, 128 * c:128 * (c + 1)],
                                             in_=psR[0:1, 128 * c:128 * (c + 1)]),
                             "sDV", f"m_swe{dt}_{c}")

                @block.tensor
                def _(te, dt=dt):
                    wt(te, f"m_swe{dt}_3")
                    if lab.get(f"m_swbe{dt - 1}") is not None:
                        wt(te, f"m_swbe{dt - 1}")
                    mark(te.matmul(out=psB[:], lhsT=ones1[:], rhs=swrow[0:1, :],
                                   start=True, stop=True), "sPE", f"m_swb{dt}")

                @block.vector
                def _(vec, dt=dt):
                    wt(vec, f"m_swb{dt}")
                    mark(vec.tensor_scalar(out=sw2b[:], in0=psB[:], scalar1=2.0,
                                           scalar2=None, op0=Alu.mult),
                         "sDV", f"m_swbe{dt}")

                for tc in range(16):
                    @block.sync
                    def _(sy, dt=dt, tc=tc):
                        gid = dt * 16 + tc
                        if gid > 1:
                            wt(sy, f"m_e1{gid - 2}")
                        dmax(sy, "sLD", hqt[gid % 2][:].rearrange("p (hc t) -> p hc t", t=128),
                             hqT_scr[tc].rearrange("hc p t -> p hc t"))
                        dmax(sy, "sLD", xsl[gid % 2][:],
                             x2_scr[128 * tc:128 * (tc + 1), 512 * dt:512 * (dt + 1)],
                             label=f"m_ld{gid}")

                    @block.tensor
                    def _(te, dt=dt, tc=tc):
                        gid = dt * 16 + tc
                        wt(te, f"m_ld{gid}")
                        wt(te, f"m_sle{(dt * 4 + 3) * 16 + 15}")
                        if gid >= 2:
                            wt(te, f"m_e1{gid - 2}")
                        for hc in range(64):
                            _li = te.matmul(out=psF[gid % 2][:],
                                            lhsT=hqt[gid % 2][:, 128 * hc:128 * (hc + 1)],
                                            rhs=slab2[:, 512 * hc:512 * (hc + 1)],
                                            start=(hc == 0), stop=(hc == 63))
                        mark(_li, "sPE", f"m_mm{gid}")

                    @block.vector
                    def _(vec, dt=dt, tc=tc):
                        gid = dt * 16 + tc
                        wt(vec, f"m_mm{gid}")
                        wt(vec, f"m_swbe{dt}")
                        if gid >= 2:
                            wt(vec, f"m_os{gid - 2}")
                        vec.scalar_tensor_tensor(out=e1b[:], in0=psF[gid % 2][:],
                                                 scalar=col("rs2neg", 1, tc),
                                                 in1=sw2b[:], op0=Alu.mult,
                                                 op1=Alu.mult)
                        mark(vec.tensor_tensor(out=otb[gid % 2][:], in0=e1b[:],
                                               in1=xsl[gid % 2][:], op=Alu.add),
                             "sDV", f"m_e1{gid}")

                    @block.gpsimd
                    def _(gp, dt=dt, tc=tc):
                        gid = dt * 16 + tc
                        wt(gp, f"m_e1{gid}")
                        dmax(gp, "sGP",
                             o_q[128 * tc:128 * (tc + 1), 512 * dt:512 * (dt + 1)],
                             otb[gid % 2][:], label=f"m_os{gid}")
            std.close()

        try:
            st4.close()
        except Exception:
            pass

        for sname in ("sLD", "sGP", "sSC"):
            fin = cnt.get(sname, 0)
            if fin:
                @block.gpsimd
                def _(gp, sname=sname, fin=fin):
                    gp.wait_ge(sem[sname], fin)

    return nc


def stage_inputs(inputs, core):
    b, g = core // 2, core % 2
    x = np.asarray(inputs["L0_attn_x"])
    m = {
        "x_full": np.ascontiguousarray(x[b]),
        "x_q": np.ascontiguousarray(x[b][qperm_rows(g)]),
        "Wq": np.asarray(inputs["L0_attn_Wq"]),
        "Wk": np.asarray(inputs["L0_attn_Wk"]),
        "Wv": np.asarray(inputs["L0_attn_Wv"]),
        "Wo": np.asarray(inputs["L0_attn_Wo"]),
        "bq_t": np.ascontiguousarray(np.asarray(inputs["L0_attn_bq"]).reshape(16, 128).T),
        "bk_t": np.ascontiguousarray(np.asarray(inputs["L0_attn_bk"]).reshape(16, 128).T),
        "bv_row": np.asarray(inputs["L0_attn_bv"]).reshape(1, D),
        "bo_row": np.asarray(inputs["L0_attn_bo"]).reshape(1, D),
        "w1": np.asarray(inputs["L0_ffn_w1"]),
        "w2": np.asarray(inputs["L0_ffn_w2"]),
        "alibi": alibi_table(),
        "maskc": mask_consts(g),
        "identd": np.eye(128, dtype=np.float32),
    }
    return {k: np.ascontiguousarray(np.asarray(v, dtype=np.float32)) for k, v in m.items()}


def kernel(**inputs):
    nc = build_program()
    in_maps = [stage_inputs(inputs, c) for c in range(8)]
    res = run_bass_kernel_spmd(nc, in_maps, core_ids=list(range(8)))
    out = np.empty((4, S, D), np.float32)
    for c in range(8):
        b, g = c // 2, c % 2
        out[b][qperm_rows(g)] = np.asarray(res.results[c]["o_q"])
    return out


# version 13 (final)
# speedup vs baseline: 1.0276x; 1.0276x over previous
import math
import numpy as np
import concourse.bass as bass
import concourse.mybir as mybir
from concourse.alu_op_type import AluOpType
from concourse.bass_utils import run_bass_kernel_spmd
from contextlib import ExitStack

F32 = mybir.dt.float32
BF16 = mybir.dt.bfloat16
AX = mybir.AxisListType.X
AF = mybir.ActivationFunctionType
Alu = AluOpType

S, D, H, HD, HID = 4096, 2048, 16, 128, 8192
NQ = 2048
NEG = -1.0e30
EPS = 1e-5

_S32 = np.float32(np.sqrt(np.float32(128.0)))
_C64 = 1.0 / np.float64(_S32)
C_HI = float(np.float32(_C64))
C_LO = float(np.float32(_C64 - np.float64(np.float32(_C64))))
_GC64 = math.sqrt(2.0 / math.pi)
GC_HI = float(np.float32(_GC64))
GC_LO = float(np.float32(_GC64 - np.float64(np.float32(_GC64))))
_GA64 = 0.044715
GA_HI = float(np.float32(_GA64))
GA_LO = float(np.float32(_GA64 - np.float64(np.float32(_GA64))))
_C164 = _GC64 * _GA64
C1_HI = float(np.float32(_C164))
C1_LO = float(np.float32(_C164 - np.float64(np.float32(_C164))))


def qperm_rows(g):
    rows = []
    for i in range(16):
        v, c01 = i // 2, i % 2
        qj = 4 * v + 2 * g + c01
        rows.extend(range(128 * qj, 128 * qj + 128))
    return np.array(rows)


_SLOPE_BITS = np.array([
    0x3f3504f3, 0x3effffff, 0x3eb504f3, 0x3e7fffff,
    0x3e3504f2, 0x3dfffffe, 0x3db504f2, 0x3d7ffffe,
    0x3d3504f1, 0x3cfffffd, 0x3cb504f1, 0x3c7ffffd,
    0x3c3504f1, 0x3bfffffc, 0x3bb504f0, 0x3b7ffffb], dtype=np.uint32)


def alibi_table():
    slopes = _SLOPE_BITS.view(np.float32)
    dist = np.arange(1 - S, 1, dtype=np.float32)
    tab = (slopes[:, None] * dist[None, :]).astype(np.float32)
    return np.ascontiguousarray(np.broadcast_to(tab[:, None, :], (16, 128, S))).copy()


def mask_consts(g):
    out = np.full((2, 128, 512), NEG, np.float32)
    for c01 in range(2):
        r = 2 * g + c01
        for q in range(128):
            out[c01, q, :128 * r + q + 1] = 0.0
    return out


def build_program(phases_upto=99):
    nc = bass.Bass("TRN2", target_bir_lowering=False, debug=False,
                   detect_race_conditions=False)

    def din(name, shape, dt=F32):
        return nc.dram_tensor(name, shape, dt, kind="ExternalInput").ap()

    x_full = din("x_full", [S, D])
    x_q = din("x_q", [NQ, D])
    Wq = din("Wq", [D, D]); Wk = din("Wk", [D, D])
    Wv = din("Wv", [D, D]); Wo = din("Wo", [D, D])
    bq_t = din("bq_t", [128, 16]); bk_t = din("bk_t", [128, 16])
    bv_row = din("bv_row", [1, D]); bo_row = din("bo_row", [1, D])
    w1 = din("w1", [HID, D]); w2 = din("w2", [D, HID])
    alibi = din("alibi", [16, 128, S])
    maskcd = din("maskc", [2, 128, 512])
    identd = din("identd", [128, 128])
    o_q = nc.dram_tensor("o_q", [NQ, D], F32, kind="ExternalOutput").ap()

    def dscr(name, shape, dt=F32):
        return nc.dram_tensor(name, shape, dt, kind="Internal").ap()

    xhf_scr = [dscr("xhf_scr_h", [16, 128, S], BF16), dscr("xhf_scr_l", [16, 128, S], BF16)]
    xhq_scr = [dscr("xhq_scr_h", [16, 128, NQ], BF16), dscr("xhq_scr_l", [16, 128, NQ], BF16)]
    q_scr = [dscr("q_scr_h", [16, 128, NQ], BF16), dscr("q_scr_l", [16, 128, NQ], BF16)]
    k_scr = [dscr("k_scr_h", [16, 128, S], BF16), dscr("k_scr_l", [16, 128, S], BF16)]
    v_scr = [dscr("v_scr_h", [S, D], BF16), dscr("v_scr_l", [S, D], BF16)]
    ctxT_scr = [dscr("ctxT_scr_h", [16, D, 128], BF16), dscr("ctxT_scr_l", [16, D, 128], BF16)]
    x2_scr = dscr("x2_scr", [NQ, D])
    v1_scr = dscr("v1_scr", [16, 128, HID])
    hqT_scr = dscr("hqT_scr", [16, 64, 128, 128], BF16)

    cnt = {}
    lab = {}

    def _inc(s, n):
        cnt[s] = cnt.get(s, 0) + n
        return cnt[s]

    with ExitStack() as outer:
        def sbT(st, name, shape, dt=F32):
            return st.enter_context(nc.sbuf_tensor(name, shape, dt))

        def psT(st, name, shape, dt=F32):
            return st.enter_context(nc.psum_tensor(name, shape, dt))

        sem = {}
        for sname in ("sLD", "sGP", "sSC", "sPE", "sDV", "sAC", "sDVt", "sACt",
                      "sGPc", "sBAR"):
            sem[sname] = outer.enter_context(nc.semaphore(sname))

        block = outer.enter_context(nc.Block())
        ENG = {"sync": block.sync, "tensor": block.tensor, "vector": block.vector,
               "scalar": block.scalar, "gpsimd": block.gpsimd}

        def mark(inst, s, label=None, n=1):
            inst.then_inc(sem[s], n)
            v = _inc(s, n)
            if label is not None:
                lab[label] = (s, v)
            return v

        def wt(e, label):
            s, v = lab[label]
            e.wait_ge(sem[s], v)

        def wto(e, label):
            if label in lab:
                wt(e, label)

        def dmax(e, q, out, in_, label=None):
            mark(e.dma_start(out=out, in_=in_), q, label, 16)

        def tick(e, inst, s="sDVt"):
            mark(inst, s)
            e.wait_ge(sem[s], cnt[s])

        def barrier():
            n = _inc("bar", 1)
            for ename, dec in ENG.items():
                @dec
                def _(e):
                    e.sem_inc(sem["sBAR"], 1)
                    e.wait_ge(sem["sBAR"], 5 * n)

        ident = sbT(outer, "ident", [128, 128])
        identb = sbT(outer, "identb", [128, 128], BF16)
        bqc = sbT(outer, "bqc", [128, 16])
        bkc = sbT(outer, "bkc", [128, 16])
        cols = sbT(outer, "cols", [128, 576])
        ones1 = sbT(outer, "ones1", [1, 128])

        CO = dict(sum1=0, sq1=32, mean1=64, rstd1=96, var1=128,
                  sumq=160, sqq=176, meanq=192, rstdq=208, varq=224,
                  sum2=240, sq2=256, mean2=272, rstd2=288, var2=304,
                  sclx=320, rsx1n=336, ghmax=352, s05=368, rs2neg=384,
                  swq=400, rsum=404, nmax=405, rnorm=406, tm=407,
                  tmpa=416, tmpb=448, tmpc=480, tmpd=512)

        def col(slot, w=1, off=0):
            return cols[:, CO[slot] + off:CO[slot] + off + w]


        @block.sync
        def _(sy):
            dmax(sy, "sLD", ident[:], identd[:])
            dmax(sy, "sLD", bqc[:], bq_t[:])
            dmax(sy, "sLD", bkc[:], bk_t[:], label="p0_ld")
        p0_ld = cnt["sLD"]

        @block.vector
        def _(vec):
            vec.wait_ge(sem["sLD"], p0_ld)
            vec.tensor_copy(out=identb[:], in_=ident[:])
            mark(vec.memset(ones1[:], 1.0), "sDV", "p0_ones")

        def emit_bcast_row(tag, tgt, row_ap, pbc):
            @block.sync
            def _(sy):
                dmax(sy, "sLD", tgt[0:1, 0:D], row_ap, label=f"{tag}_row")
            for j in range(4):
                @block.tensor
                def _(te, j=j):
                    wt(te, "p0_ones")
                    wt(te, f"{tag}_row")
                    wto(te, f"{tag}_bc{j - 1}")
                    mark(te.matmul(out=pbc[:], lhsT=ones1[:],
                                   rhs=tgt[0:1, 512 * j:512 * (j + 1)],
                                   start=True, stop=True), "sPE", f"{tag}_mm{j}")

                @block.vector
                def _(vec, j=j):
                    wt(vec, f"{tag}_mm{j}")
                    mark(vec.tensor_copy(out=tgt[:, 512 * j:512 * (j + 1)], in_=pbc[:]),
                         "sDV", f"{tag}_bc{j}")
        barrier()

        def p1a(tag, src_ap, ntiles, sl_sum, sl_sq):
            st = ExitStack()
            bufs = [sbT(st, f"{tag}_xa", [128, D]), sbT(st, f"{tag}_xb", [128, D])]
            junk = sbT(st, f"{tag}_junk", [128, D])
            for t in range(ntiles):
                @block.sync
                def _(sy, t=t):
                    if t >= 2:
                        wt(sy, f"{tag}_sq{t - 2}")
                        wt(sy, f"{tag}_sum{t - 2}")
                    dmax(sy, "sLD", bufs[t % 2][:],
                         src_ap[128 * t:128 * (t + 1), :], label=f"{tag}_x{t}")

                @block.scalar
                def _(sc, t=t):
                    wt(sc, f"{tag}_x{t}")
                    mark(sc.activation(out=junk[:], in_=bufs[t % 2][:], func=AF.Square,
                                       accum_out=col(sl_sq, 1, t)),
                         "sAC", f"{tag}_sq{t}")

                @block.vector
                def _(vec, t=t):
                    wt(vec, f"{tag}_x{t}")
                    mark(vec.reduce_sum(out=col(sl_sum, 1, t), in_=bufs[t % 2][:],
                                        axis=AX), "sDV", f"{tag}_sum{t}")
            st.close()

        def p1a_colmath(tag, n, sl_sum, sl_sq, sl_mean, sl_var, sl_rstd, inv_n,
                        wait_sq, wait_sum):
            @block.vector
            def _(vec):
                wt(vec, wait_sq)
                wt(vec, wait_sum)
                tick(vec, vec.tensor_scalar(out=col(sl_mean, n), in0=col(sl_sum, n),
                                            scalar1=inv_n, scalar2=None, op0=Alu.mult))
                tick(vec, vec.tensor_scalar(out=col("tmpa", n), in0=col(sl_sq, n),
                                            scalar1=inv_n, scalar2=None, op0=Alu.mult))
                tick(vec, vec.tensor_tensor(out=col("tmpb", n), in0=col(sl_mean, n),
                                            in1=col(sl_mean, n), op=Alu.mult))
                mark(vec.tensor_tensor(out=col(sl_var, n), in0=col("tmpa", n),
                                       in1=col("tmpb", n), op=Alu.subtract),
                     "sDV", f"{tag}_var")

            @block.vector
            def _(vec):
                wt(vec, f"{tag}_var")
                mark(vec.tensor_scalar(out=col("tmpd", n), in0=col(sl_var, n),
                                       scalar1=EPS, scalar2=None, op0=Alu.add),
                     "sDV", f"{tag}_vareps")

            @block.scalar
            def _(sc):
                wt(sc, f"{tag}_vareps")
                mark(sc.activation(out=col("tmpc", n), in_=col("tmpd", n),
                                   func=AF.Sqrt),
                     "sAC", f"{tag}_sd")

            @block.vector
            def _(vec):
                wt(vec, f"{tag}_sd")
                mark(vec.reciprocal(out=col(sl_rstd, n), in_=col("tmpc", n)),
                     "sDV", f"{tag}_rstd")

        if phases_upto >= 1:
            p1a("lf", x_full, 32, "sum1", "sq1")
            p1a("lq", x_q, 16, "sumq", "sqq")
            p1a_colmath("lf", 32, "sum1", "sq1", "mean1", "var1", "rstd1",
                        1.0 / 2048, "lf_sq31", "lf_sum31")
            p1a_colmath("lq", 16, "sumq", "sqq", "meanq", "varq", "rstdq",
                        1.0 / 2048, "lq_sq15", "lq_sum15")
            barrier()

        def p1b(tag, src_ap, dst_scr, ntiles, sl_mean, sl_rstd, rstd_lbl):
            st = ExitStack()
            bufs = [sbT(st, f"{tag}_a", [128, D]), sbT(st, f"{tag}_b", [128, D])]
            xh = sbT(st, f"{tag}_h", [128, D])
            xhl = [sbT(st, f"{tag}_s0", [128, D], BF16), sbT(st, f"{tag}_s1", [128, D], BF16)]
            fmg = [sbT(st, f"{tag}_fm0", [128, 8192], BF16), sbT(st, f"{tag}_fm1", [128, 8192], BF16)]
            pst = [psT(st, f"{tag}_p0", [128, 512], BF16), psT(st, f"{tag}_p1", [128, 512], BF16)]
            ngrp = ntiles // 4
            for grp in range(ngrp):
                for tl in range(4):
                    t = grp * 4 + tl

                    @block.sync
                    def _(sy, t=t):
                        if t >= 2:
                            wt(sy, f"{tag}_xh{t - 1}")
                        dmax(sy, "sLD", bufs[t % 2][:],
                             src_ap[128 * t:128 * (t + 1), :], label=f"{tag}_x{t}")

                    @block.vector
                    def _(vec, t=t, grp=grp, tl=tl):
                        wt(vec, f"{tag}_x{t}")
                        if t == 0:
                            wt(vec, rstd_lbl)
                        if tl == 0 and grp > 0:
                            wt(vec, f"{tag}_spill{grp - 1}")
                        vec.tensor_scalar(out=xh[:], in0=bufs[t % 2][:],
                                          scalar1=col(sl_mean, 1, t),
                                          scalar2=col(sl_rstd, 1, t),
                                          op0=Alu.subtract, op1=Alu.mult)
                        vec.tensor_copy(out=xhl[0][:], in_=xh[:])
                        mark(vec.tensor_tensor(out=xhl[1][:], in0=xh[:], in1=xhl[0][:],
                                               op=Alu.subtract),
                             "sDV", f"{tag}_xh{t}")

                    for hl in range(2):
                        for qd in range(4):
                            @block.tensor
                            def _(te, t=t, qd=qd, hl=hl):
                                wt(te, f"{tag}_xh{t}")
                                gq = _inc(f"{tag}_q", 1) - 1
                                if gq >= 2:
                                    wt(te, f"{tag}_fme{gq - 2}")
                                for u in range(4):
                                    dc = qd * 4 + u
                                    _li = te.transpose(
                                        out=pst[gq % 2][:, 128 * u:128 * (u + 1)],
                                        in_=xhl[hl][:, 128 * dc:128 * (dc + 1)],
                                        identity=identb[:])
                                mark(_li, "sPE", f"{tag}_tq{gq}")

                            @block.vector
                            def _(vec, t=t, qd=qd, tl=tl, hl=hl):
                                gq = _inc(f"{tag}_qe", 1) - 1
                                wt(vec, f"{tag}_tq{gq}")
                                fmr = fmg[hl][:].rearrange("p (dc tl c) -> p dc tl c",
                                                           dc=16, tl=4)
                                mark(vec.tensor_copy(
                                    out=fmr[:, 4 * qd:4 * qd + 4, tl, :],
                                    in_=pst[gq % 2][:].rearrange("p (u c) -> p u c", u=4)),
                                    "sDV", f"{tag}_fme{gq}")

                @block.gpsimd
                def _(gp, grp=grp):
                    wt(gp, f"{tag}_fme{_inc(f'{tag}_qe', 0) - 1}")
                    for hl in range(2):
                        for dc in range(16):
                            last = hl == 1 and dc == 15
                            dmax(gp, "sGP", dst_scr[hl][dc][:, 512 * grp:512 * (grp + 1)],
                                 fmg[hl][:, 512 * dc:512 * (dc + 1)],
                                 label=f"{tag}_spill{grp}" if last else None)
            st.close()

        if phases_upto >= 2:
            p1b("xf", x_full, xhf_scr, 32, "mean1", "rstd1", "lf_rstd")
            barrier()
            p1b("xq", x_q, xhq_scr, 16, "meanq", "rstdq", "lq_rstd")
            barrier()

        def load_w_split(tag, wdram, st):
            whb = sbT(st, f"{tag}_wh", [128, 32768], BF16)
            wlb = sbT(st, f"{tag}_wl", [128, 32768], BF16)
            wstg = [sbT(st, f"{tag}_ws0", [128, D]), sbT(st, f"{tag}_ws1", [128, D])]
            for dc in range(16):
                @block.sync
                def _(sy, dc=dc):
                    if dc >= 2:
                        wt(sy, f"{tag}_wsp{dc - 2}")
                    dmax(sy, "sLD", wstg[dc % 2][:],
                         wdram[128 * dc:128 * (dc + 1), :], label=f"{tag}_wld{dc}")

                @block.vector
                def _(vec, dc=dc):
                    wt(vec, f"{tag}_wld{dc}")
                    vec.tensor_copy(out=whb[:, 2048 * dc:2048 * (dc + 1)],
                                    in_=wstg[dc % 2][:])
                    mark(vec.scalar_tensor_tensor(
                        out=wlb[:, 2048 * dc:2048 * (dc + 1)], in0=wstg[dc % 2][:],
                        scalar=1.0, in1=whb[:, 2048 * dc:2048 * (dc + 1)],
                        op0=Alu.mult, op1=Alu.subtract),
                        "sDV", f"{tag}_wsp{dc}" if True else None)
            lab[f"{tag}_w"] = lab[f"{tag}_wsp15"]
            return whb, wlb

        def load_xg(tag, grp, src_scr, xgh, xgl, prev_lbl):
            @block.sync
            def _(sy):
                if prev_lbl:
                    wt(sy, prev_lbl)
                for hl, xg in ((0, xgh), (1, xgl)):
                    for dc in range(16):
                        dmax(sy, "sLD", xg[:, 512 * dc:512 * (dc + 1)],
                             src_scr[hl][dc][:, 512 * grp:512 * (grp + 1)],
                             label=f"{tag}_xg{grp}" if hl == 1 and dc == 15 else None)

        def mm3(te, psum, wh, wl, xh, xl, first, last):
            te.matmul(out=psum, lhsT=wh, rhs=xh, start=first, stop=False)
            te.matmul(out=psum, lhsT=wl, rhs=xh, start=False, stop=False)
            return te.matmul(out=psum, lhsT=wh, rhs=xl, start=False, stop=last)

        def p1c_ksweep(st):
            whb, wlb = load_w_split("k", Wk, st)
            xgh = sbT(st, "kxgh", [128, 8192], BF16)
            xgl = sbT(st, "kxgl", [128, 8192], BF16)
            kob = [[sbT(st, f"ko{i}{hl}", [128, 512], BF16) for hl in range(2)]
                   for i in range(2)]
            pp = [psT(st, "kp0", [128, 512]), psT(st, "kp1", [128, 512])]
            for grp in range(8):
                load_xg("k", grp, xhf_scr, xgh, xgl,
                        f"k_mm{grp - 1}_15" if grp > 0 else None)
                for h in range(16):
                    @block.tensor
                    def _(te, grp=grp, h=h):
                        wt(te, f"k_xg{grp}")
                        if grp == 0 and h == 0:
                            wt(te, "k_w")
                        prev = grp * 16 + h - 2
                        if prev >= 0:
                            wt(te, f"k_ev{prev}")
                        for dc in range(16):
                            wsl = slice(2048 * dc + 128 * h, 2048 * dc + 128 * (h + 1))
                            xsl = slice(512 * dc, 512 * (dc + 1))
                            _li = mm3(te, pp[h % 2][:], whb[:, wsl], wlb[:, wsl],
                                      xgh[:, xsl], xgl[:, xsl], dc == 0, dc == 15)
                        mark(_li, "sPE", f"k_mm{grp}_{h}")

                    @block.vector
                    def _(vec, grp=grp, h=h):
                        wt(vec, f"k_mm{grp}_{h}")
                        prev = grp * 16 + h - 2
                        if prev >= 0:
                            wt(vec, f"k_st{prev}")
                        vec.tensor_scalar(out=kob[h % 2][0][:], in0=pp[h % 2][:],
                                          scalar1=bkc[:, h:h + 1], scalar2=None,
                                          op0=Alu.add)
                        mark(vec.scalar_tensor_tensor(out=kob[h % 2][1][:],
                                                      in0=pp[h % 2][:],
                                                      scalar=bkc[:, h:h + 1],
                                                      in1=kob[h % 2][0][:],
                                                      op0=Alu.add, op1=Alu.subtract),
                             "sDV", f"k_ev{grp * 16 + h}")

                    @block.gpsimd
                    def _(gp, grp=grp, h=h):
                        wt(gp, f"k_ev{grp * 16 + h}")
                        dmax(gp, "sGP", k_scr[0][h][:, 512 * grp:512 * (grp + 1)],
                             kob[h % 2][0][:])
                        dmax(gp, "sGP", k_scr[1][h][:, 512 * grp:512 * (grp + 1)],
                             kob[h % 2][1][:], label=f"k_st{grp * 16 + h}")

        def p1d_vsweep(st):
            bv_b = sbT(st, "bv_b", [128, D])
            pbc_v = psT(st, "pbc_v", [128, 512])
            emit_bcast_row("bvb", bv_b, bv_row[:], pbc_v)
            whb, wlb = load_w_split("v", Wv, st)
            xgh = sbT(st, "vxgh", [128, 8192], BF16)
            xgl = sbT(st, "vxgl", [128, 8192], BF16)
            vf = sbT(st, "vf", [128, 512])
            vob = [[sbT(st, f"vo{i}{hl}", [128, 512], BF16) for hl in range(2)]
                   for i in range(2)]
            pp = [psT(st, "vp0", [128, 512]), psT(st, "vp1", [128, 512])]
            for grp in range(8):
                load_xg("v", grp, xhf_scr, xgh, xgl,
                        f"v_mm{grp - 1}_15" if grp > 0 else None)
                for sub in range(16):
                    tch, hdt = sub // 4, sub % 4

                    @block.tensor
                    def _(te, grp=grp, sub=sub, tch=tch, hdt=hdt):
                        wt(te, f"v_xg{grp}")
                        if grp == 0 and sub == 0:
                            wt(te, "v_w")
                            wt(te, "bvb_bc3")
                        prev = grp * 16 + sub - 2
                        if prev >= 0:
                            wt(te, f"v_ev{prev}")
                        for dc in range(16):
                            xsl = slice(512 * dc + 128 * tch, 512 * dc + 128 * (tch + 1))
                            wsl = slice(2048 * dc + 512 * hdt, 2048 * dc + 512 * (hdt + 1))
                            _li = mm3(te, pp[sub % 2][:], xgh[:, xsl], xgl[:, xsl],
                                      whb[:, wsl], wlb[:, wsl], dc == 0, dc == 15)
                        mark(_li, "sPE", f"v_mm{grp}_{sub}")

                    @block.vector
                    def _(vec, grp=grp, sub=sub, hdt=hdt):
                        wt(vec, f"v_mm{grp}_{sub}")
                        prev = grp * 16 + sub - 2
                        if prev >= 0:
                            wt(vec, f"v_st{prev}")
                        vec.tensor_tensor(out=vf[:], in0=pp[sub % 2][:],
                                          in1=bv_b[:, 512 * hdt:512 * (hdt + 1)],
                                          op=Alu.add)
                        vec.tensor_copy(out=vob[sub % 2][0][:], in_=vf[:])
                        mark(vec.tensor_tensor(out=vob[sub % 2][1][:], in0=vf[:],
                                               in1=vob[sub % 2][0][:], op=Alu.subtract),
                             "sDV", f"v_ev{grp * 16 + sub}")

                    @block.gpsimd
                    def _(gp, grp=grp, sub=sub, tch=tch, hdt=hdt):
                        wt(gp, f"v_ev{grp * 16 + sub}")
                        r0 = 512 * grp + 128 * tch
                        dmax(gp, "sGP", v_scr[0][r0:r0 + 128, 512 * hdt:512 * (hdt + 1)],
                             vob[sub % 2][0][:])
                        dmax(gp, "sGP", v_scr[1][r0:r0 + 128, 512 * hdt:512 * (hdt + 1)],
                             vob[sub % 2][1][:], label=f"v_st{grp * 16 + sub}")

        def p1e_qsweep(st):
            whb, wlb = load_w_split("q", Wq, st)
            xgh = sbT(st, "qxgh", [128, 8192], BF16)
            xgl = sbT(st, "qxgl", [128, 8192], BF16)
            qtmp = sbT(st, "qtmp", [128, 512])
            qtmp2 = sbT(st, "qtmp2", [128, 512])
            qv = sbT(st, "qv", [128, 512])
            qob = [[sbT(st, f"qo{i}{hl}", [128, 512], BF16) for hl in range(2)]
                   for i in range(2)]
            pp = [psT(st, "qp0", [128, 512]), psT(st, "qp1", [128, 512])]
            for grp in range(4):
                load_xg("q", grp, xhq_scr, xgh, xgl,
                        f"q_mm{grp - 1}_15" if grp > 0 else None)
                for h in range(16):
                    @block.tensor
                    def _(te, grp=grp, h=h):
                        wt(te, f"q_xg{grp}")
                        if grp == 0 and h == 0:
                            wt(te, "q_w")
                        prev = grp * 16 + h - 2
                        if prev >= 0:
                            wt(te, f"q_ev{prev}")
                        for dc in range(16):
                            wsl = slice(2048 * dc + 128 * h, 2048 * dc + 128 * (h + 1))
                            xsl = slice(512 * dc, 512 * (dc + 1))
                            _li = mm3(te, pp[h % 2][:], whb[:, wsl], wlb[:, wsl],
                                      xgh[:, xsl], xgl[:, xsl], dc == 0, dc == 15)
                        mark(_li, "sPE", f"q_mm{grp}_{h}")

                    @block.vector
                    def _(vec, grp=grp, h=h):
                        wt(vec, f"q_mm{grp}_{h}")
                        prev = grp * 16 + h - 2
                        if prev >= 0:
                            wt(vec, f"q_st{prev}")
                        vec.tensor_scalar(out=qtmp[:], in0=pp[h % 2][:],
                                          scalar1=bqc[:, h:h + 1], scalar2=None,
                                          op0=Alu.add)
                        vec.tensor_scalar(out=qtmp2[:], in0=qtmp[:],
                                          scalar1=C_HI, scalar2=None, op0=Alu.mult)
                        vec.scalar_tensor_tensor(out=qv[:], in0=qtmp[:],
                                                 scalar=C_LO, in1=qtmp2[:],
                                                 op0=Alu.mult, op1=Alu.add)
                        vec.tensor_copy(out=qob[h % 2][0][:], in_=qv[:])
                        mark(vec.tensor_tensor(out=qob[h % 2][1][:], in0=qv[:],
                                               in1=qob[h % 2][0][:], op=Alu.subtract),
                             "sDV", f"q_ev{grp * 16 + h}")

                    @block.gpsimd
                    def _(gp, grp=grp, h=h):
                        wt(gp, f"q_ev{grp * 16 + h}")
                        dmax(gp, "sGP", q_scr[0][h][:, 512 * grp:512 * (grp + 1)],
                             qob[h % 2][0][:])
                        dmax(gp, "sGP", q_scr[1][h][:, 512 * grp:512 * (grp + 1)],
                             qob[h % 2][1][:], label=f"q_st{grp * 16 + h}")

        if phases_upto >= 3:
            st = ExitStack(); p1c_ksweep(st); st.close()
            barrier()
            st = ExitStack(); p1d_vsweep(st); st.close()
            barrier()
            st = ExitStack(); p1e_qsweep(st); st.close()
            barrier()

        def p2(st):
            maskb = sbT(st, "maskb", [128, 1024])
            kb2 = [[sbT(st, f"kb{i}{hl}", [128, S], BF16) for hl in range(2)]
                   for i in range(2)]
            qb2 = [[sbT(st, f"qb{i}{hl}", [128, NQ], BF16) for hl in range(2)]
                   for i in range(2)]
            vb2 = [[sbT(st, f"vb{i}{hl}", [128, S], BF16) for hl in range(2)]
                   for i in range(2)]
            ab2 = [sbT(st, "ab0", [128, S]), sbT(st, "ab1", [128, S])]
            srow2 = [sbT(st, "srowA", [128, S]), sbT(st, "srowB", [128, S])]
            prow2 = [sbT(st, "prowA", [128, S]), sbT(st, "prowB", [128, S])]
            ptb = [[sbT(st, f"ptb{i}{hl}", [128, 512], BF16) for hl in range(2)]
                   for i in range(2)]
            ctb0 = [sbT(st, "ctb0A", [128, 256], BF16), sbT(st, "ctb0B", [128, 256], BF16)]
            ctb1 = [sbT(st, "ctb1A", [128, 256], BF16), sbT(st, "ctb1B", [128, 256], BF16)]
            psS = [psT(st, "psS0", [128, 512]), psT(st, "psS1", [128, 512])]
            pst = [psT(st, "psT0", [128, 512]), psT(st, "psT1", [128, 512])]
            psC = [psT(st, "psC0", [128, 128]), psT(st, "psC1", [128, 128])]
            psX = [psT(st, "psX0", [128, 256], BF16), psT(st, "psX1", [128, 256], BF16)]
            SB0 = 544

            def scol(name, stp):
                off = SB0 + {"rsum": 0, "nmax": 2, "rnorm": 4}[name] * 2 + stp
                return cols[:, off:off + 1]

            @block.sync
            def _(sy):
                dmax(sy, "sLD", maskb[:, 0:512], maskcd[0])
                dmax(sy, "sLD", maskb[:, 512:1024], maskcd[1], label="a_mask")

            def emit_loads(h):
                kb, qb, vb, ab = kb2[h % 2], qb2[h % 2], vb2[h % 2], ab2[h % 2]

                @block.sync
                def _(sy, h=h, kb=kb, qb=qb, vb=vb, ab=ab):
                    if h > 1:
                        wt(sy, f"a_ctxs{h - 2}_15")
                    for hl in range(2):
                        dmax(sy, "sLD", kb[hl][:], k_scr[hl][h][:, :])
                        dmax(sy, "sLD", qb[hl][:], q_scr[hl][h][:, :])
                        dmax(sy, "sLD", vb[hl][:].rearrange("p (kc c) -> p kc c", c=128),
                             v_scr[hl].rearrange("(kc p) f -> p kc f", p=128)[
                                 :, :, 128 * h:128 * (h + 1)])
                    dmax(sy, "sLD", ab[:], alibi[h], label=f"a_ld{h}")

            def emit_attn(h, i):
                stp = h % 2
                kb, qb, vb, ab = kb2[stp], qb2[stp], vb2[stp], ab2[stp]
                srow, prow = srow2[stp], prow2[stp]
                KT = i // 2 + 1
                L = 512 * KT
                c01 = i % 2
                qsl = slice(128 * i, 128 * (i + 1))
                ii = h * 16 + i
                ph, pi = (h, i - 1) if i > 0 else (h - 2, 15)
                for kt in range(KT):
                    @block.tensor
                    def _(te, h=h, i=i, kt=kt, kb=kb, qb=qb, qsl=qsl):
                        wt(te, f"a_ld{h}")
                        if h == 0 and i == 0 and kt == 0:
                            wt(te, "a_mask")
                        gid = _inc("a_smm", 1) - 1
                        if gid >= 2:
                            wt(te, f"a_sev{gid - 2}")
                        ksl = slice(512 * kt, 512 * (kt + 1))
                        _li = mm3(te, psS[gid % 2][:], qb[0][:, qsl], qb[1][:, qsl],
                                  kb[0][:, ksl], kb[1][:, ksl], True, True)
                        mark(_li, "sPE", f"a_smm{gid}")

                    @block.vector
                    def _(vec, h=h, i=i, kt=kt, KT=KT, c01=c01, ab=ab, srow=srow,
                          ph=ph, pi=pi):
                        gid = _inc("a_sev_c", 1) - 1
                        wt(vec, f"a_smm{gid}")
                        if kt == 0 and lab.get(f"a_exp{ph}_{pi}") is not None:
                            wt(vec, f"a_exp{ph}_{pi}")
                        mark(vec.tensor_tensor(out=srow[:, 512 * kt:512 * (kt + 1)],
                                               in0=psS[gid % 2][:],
                                               in1=ab[:, 512 * kt:512 * (kt + 1)],
                                               op=Alu.add),
                             "sDV", f"a_sev{gid}")
                        if kt == KT - 1:
                            mark(vec.tensor_tensor(
                                out=srow[:, 512 * kt:512 * (kt + 1)],
                                in0=srow[:, 512 * kt:512 * (kt + 1)],
                                in1=maskb[:, 512 * c01:512 * (c01 + 1)],
                                op=Alu.add), "sDV", f"a_msk{h}_{i}")

                @block.vector
                def _(vec, h=h, i=i, L=L, srow=srow, stp=stp):
                    wt(vec, f"a_msk{h}_{i}")
                    mark(vec.reduce_max(out=scol("nmax", stp), in_=srow[:, :L],
                                        axis=AX, negate=True),
                         "sDV", f"a_nm{h}_{i}")

                @block.scalar
                def _(sc, h=h, i=i, L=L, srow=srow, prow=prow, stp=stp, ph=ph, pi=pi):
                    wt(sc, f"a_nm{h}_{i}")
                    if lab.get(f"a_avmm{ph}_{pi}_last") is not None:
                        wt(sc, f"a_avmm{ph}_{pi}_last")
                    mark(sc.activation(out=prow[:, :L], in_=srow[:, :L], func=AF.Exp,
                                       bias=scol("nmax", stp), scale=1.0,
                                       accum_out=scol("rsum", stp)),
                         "sAC", f"a_exp{h}_{i}")

                @block.vector
                def _(vec, h=h, i=i, stp=stp):
                    wt(vec, f"a_exp{h}_{i}")
                    mark(vec.reciprocal(out=scol("rnorm", stp), in_=scol("rsum", stp)),
                         "sDV", f"a_rn{h}_{i}")

                for qk in range(KT):
                    @block.tensor
                    def _(te, h=h, i=i, qk=qk, prow=prow):
                        wt(te, f"a_exp{h}_{i}")
                        gid = _inc("a_ptq", 1) - 1
                        if gid >= 2:
                            wt(te, f"a_pte{gid - 2}")
                        for u in range(4):
                            kc = 4 * qk + u
                            _li = te.transpose(out=pst[gid % 2][:, 128 * u:128 * (u + 1)],
                                               in_=prow[:, 128 * kc:128 * (kc + 1)],
                                               identity=ident[:])
                        mark(_li, "sPE", f"a_pt{gid}")

                    @block.vector
                    def _(vec, h=h, i=i, qk=qk):
                        gid = _inc("a_pte_c", 1) - 1
                        wt(vec, f"a_pt{gid}")
                        if gid >= 2:
                            wt(vec, f"a_avg{gid - 2}")
                        vec.tensor_copy(out=ptb[gid % 2][0][:], in_=pst[gid % 2][:])
                        mark(vec.scalar_tensor_tensor(out=ptb[gid % 2][1][:],
                                                      in0=pst[gid % 2][:],
                                                      scalar=1.0,
                                                      in1=ptb[gid % 2][0][:],
                                                      op0=Alu.mult,
                                                      op1=Alu.subtract),
                             "sDV", f"a_pte{gid}")

                    @block.tensor
                    def _(te, h=h, i=i, qk=qk, KT=KT, vb=vb, ii=ii):
                        gid = _inc("a_avq", 1) - 1
                        wt(te, f"a_pte{gid}")
                        if qk == 0:
                            zh, zi = (h, i - 2) if i > 1 else (h - 2, 14 + i)
                            if lab.get(f"a_ctxn{zh}_{zi}") is not None:
                                wt(te, f"a_ctxn{zh}_{zi}")
                        for u in range(4):
                            kc = 4 * qk + u
                            vsl = slice(128 * kc, 128 * (kc + 1))
                            psl = slice(128 * u, 128 * (u + 1))
                            te.matmul(out=psC[ii % 2][:], lhsT=ptb[gid % 2][0][:, psl],
                                      rhs=vb[0][:, vsl],
                                      start=(qk == 0 and u == 0), stop=False)
                            te.matmul(out=psC[ii % 2][:], lhsT=ptb[gid % 2][1][:, psl],
                                      rhs=vb[0][:, vsl], start=False, stop=False)
                            _li = te.matmul(out=psC[ii % 2][:], lhsT=ptb[gid % 2][0][:, psl],
                                            rhs=vb[1][:, vsl], start=False,
                                            stop=(qk == KT - 1 and u == 3))
                        lbl = f"a_avmm{h}_{i}_last" if qk == KT - 1 else f"a_avg_m{gid}"
                        mark(_li, "sPE", lbl)
                        lab[f"a_avg{gid}"] = ("sPE", cnt["sPE"])

                @block.vector
                def _(vec, h=h, i=i, stp=stp, ii=ii, ph=ph, pi=pi):
                    wt(vec, f"a_avmm{h}_{i}_last")
                    wt(vec, f"a_rn{h}_{i}")
                    if lab.get(f"a_ctT{ph}_{pi}") is not None:
                        wt(vec, f"a_ctT{ph}_{pi}")
                    vec.tensor_scalar(out=ctb0[stp][:, 0:128], in0=psC[ii % 2][:],
                                      scalar1=scol("rnorm", stp), scalar2=None,
                                      op0=Alu.mult)
                    mark(vec.scalar_tensor_tensor(out=ctb0[stp][:, 128:256],
                                                  in0=psC[ii % 2][:],
                                                  scalar=scol("rnorm", stp),
                                                  in1=ctb0[stp][:, 0:128],
                                                  op0=Alu.mult, op1=Alu.subtract),
                         "sDV", f"a_ctxn{h}_{i}")

                @block.tensor
                def _(te, h=h, i=i, stp=stp, ph=ph, pi=pi):
                    wt(te, f"a_ctxn{h}_{i}")
                    if lab.get(f"a_ctxe{ph}_{pi}") is not None:
                        wt(te, f"a_ctxe{ph}_{pi}")
                    te.transpose(out=psX[stp][:, 0:128], in_=ctb0[stp][:, 0:128],
                                 identity=identb[:])
                    mark(te.transpose(out=psX[stp][:, 128:256], in_=ctb0[stp][:, 128:256],
                                      identity=identb[:]),
                         "sPE", f"a_ctT{h}_{i}")

                @block.vector
                def _(vec, h=h, i=i, stp=stp, ph=ph, pi=pi):
                    wt(vec, f"a_ctT{h}_{i}")
                    if lab.get(f"a_ctxs{ph}_{pi}") is not None:
                        wt(vec, f"a_ctxs{ph}_{pi}")
                    mark(vec.tensor_copy(out=ctb1[stp][:], in_=psX[stp][:]),
                         "sDV", f"a_ctxe{h}_{i}")

                @block.gpsimd
                def _(gp, h=h, i=i, stp=stp):
                    wt(gp, f"a_ctxe{h}_{i}")
                    dmax(gp, "sGP", ctxT_scr[0][i][128 * h:128 * (h + 1), :],
                         ctb1[stp][:, 0:128])
                    dmax(gp, "sGP", ctxT_scr[1][i][128 * h:128 * (h + 1), :],
                         ctb1[stp][:, 128:256], label=f"a_ctxs{h}_{i}")

            for hp in range(8):
                emit_loads(2 * hp)
                emit_loads(2 * hp + 1)
                for i in range(16):
                    emit_attn(2 * hp, i)
                    emit_attn(2 * hp + 1, i)

        if phases_upto >= 4:
            st = ExitStack(); p2(st); st.close()
            barrier()

        def p3(st):
            bo_b = sbT(st, "bo_b", [128, D])
            pbc_o = psT(st, "pbc_o", [128, 512])
            emit_bcast_row("bob", bo_b, bo_row[:], pbc_o)
            woh, wol = load_w_split("o", Wo, st)
            ctq = [sbT(st, "ctqh", [128, 2048], BF16), sbT(st, "ctql", [128, 2048], BF16)]
            xqt = sbT(st, "xqt", [128, D])
            xbo = sbT(st, "xbo", [128, D])
            x2t = sbT(st, "x2t", [128, D])
            psO = [psT(st, "psO0", [128, 512]), psT(st, "psO1", [128, 512])]
            for qc in range(16):
                @block.sync
                def _(sy, qc=qc):
                    if qc > 0:
                        wt(sy, f"o_mm{qc - 1}_3")
                        wt(sy, f"o_x2s{qc - 1}")
                    for hl in range(2):
                        dmax(sy, "sLD",
                             ctq[hl][:].rearrange("p (fc q) -> p fc q", q=128),
                             ctxT_scr[hl][qc].rearrange("(fc p) q -> p fc q", p=128))
                    dmax(sy, "sLD", xqt[:], x_q[128 * qc:128 * (qc + 1), :],
                         label=f"o_ld{qc}")

                @block.vector
                def _(vec, qc=qc):
                    wt(vec, f"o_ld{qc}")
                    if qc == 0:
                        wt(vec, "bob_bc3")
                    mark(vec.tensor_tensor(out=xbo[:], in0=xqt[:], in1=bo_b[:],
                                           op=Alu.add), "sDV", f"o_xbo{qc}")

                for dt in range(4):
                    @block.tensor
                    def _(te, qc=qc, dt=dt):
                        wt(te, f"o_ld{qc}")
                        if qc == 0 and dt == 0:
                            wt(te, "o_w")
                        gid = qc * 4 + dt
                        if gid >= 2:
                            wt(te, f"o_ev{gid - 2}")
                        for fc in range(16):
                            csl = slice(128 * fc, 128 * (fc + 1))
                            wsl = slice(2048 * fc + 512 * dt, 2048 * fc + 512 * (dt + 1))
                            _li = mm3(te, psO[gid % 2][:], ctq[0][:, csl], ctq[1][:, csl],
                                      woh[:, wsl], wol[:, wsl], fc == 0, fc == 15)
                        mark(_li, "sPE", f"o_mm{qc}_{dt}")

                    @block.vector
                    def _(vec, qc=qc, dt=dt):
                        gid = qc * 4 + dt
                        wt(vec, f"o_mm{qc}_{dt}")
                        wt(vec, f"o_xbo{qc}")
                        mark(vec.tensor_tensor(out=x2t[:, 512 * dt:512 * (dt + 1)],
                                               in0=psO[gid % 2][:],
                                               in1=xbo[:, 512 * dt:512 * (dt + 1)],
                                               op=Alu.add),
                             "sDV", f"o_ev{gid}")

                @block.scalar
                def _(sc, qc=qc):
                    wt(sc, f"o_ev{qc * 4 + 3}")
                    mark(sc.activation(out=xqt[:], in_=x2t[:], func=AF.Square,
                                       accum_out=col("sq2", 1, qc)),
                         "sAC", f"o_sq{qc}")

                @block.vector
                def _(vec, qc=qc):
                    wt(vec, f"o_ev{qc * 4 + 3}")
                    mark(vec.reduce_sum(out=col("sum2", 1, qc), in_=x2t[:], axis=AX),
                         "sDV", f"o_sum{qc}")

                @block.gpsimd
                def _(gp, qc=qc):
                    wt(gp, f"o_sq{qc}")
                    wt(gp, f"o_sum{qc}")
                    dmax(gp, "sGP", x2_scr[128 * qc:128 * (qc + 1), :], x2t[:],
                         label=f"o_x2s{qc}")

        if phases_upto >= 5:
            st = ExitStack(); p3(st); st.close()
            p1a_colmath("l2", 16, "sum2", "sq2", "mean2", "var2", "rstd2",
                        1.0 / 2048, "o_sq15", "o_sum15")
            barrier()

        st4 = ExitStack()
        if phases_upto >= 6:
            xqT = sbT(st4, "xqT", [128, 32768], BF16)
            sta = ExitStack()
            xa = sbT(sta, "f_xa", [128, D])
            xh2 = sbT(sta, "f_xh2", [128, D])
            ub = sbT(sta, "f_u", [128, D])
            vvb = sbT(sta, "f_vv", [128, D])
            bb = sbT(sta, "f_bb", [128, D])
            xqbf = sbT(sta, "f_xqbf", [128, D], BF16)
            psq = [psT(sta, "f_ps0", [128, 512], BF16), psT(sta, "f_ps1", [128, 512], BF16)]
            for tc in range(16):
                @block.sync
                def _(sy, tc=tc):
                    if tc > 0:
                        wt(sy, f"f_u{tc - 1}")
                    dmax(sy, "sLD", xa[:], x2_scr[128 * tc:128 * (tc + 1), :],
                         label=f"f_x2{tc}")

                @block.vector
                def _(vec, tc=tc):
                    wt(vec, f"f_x2{tc}")
                    if tc == 0:
                        wt(vec, "l2_rstd")
                    if tc > 0:
                        wt(vec, f"f_xqe{tc - 1}_3")
                        wt(vec, f"f_xqt{tc - 1}_3")
                    vec.tensor_scalar(out=xh2[:], in0=xa[:],
                                      scalar1=col("mean2", 1, tc),
                                      scalar2=col("rstd2", 1, tc),
                                      op0=Alu.subtract, op1=Alu.mult)
                    tick(vec, vec.reduce_max(out=col("tm"), in_=xh2[:], axis=AX,
                                             apply_absolute_value=True))
                    tick(vec, vec.tensor_scalar(out=col("tmpa"), in0=col("tm"),
                                                scalar1=1e-5, scalar2=None, op0=Alu.max))
                    tick(vec, vec.reciprocal(out=col("tmpb"), in_=col("tmpa")))
                    tick(vec, vec.tensor_scalar(out=col("sclx", 1, tc), in0=col("tmpb"),
                                                scalar1=127.0, scalar2=None, op0=Alu.mult))
                    tick(vec, vec.reciprocal(out=col("tmpc"), in_=col("sclx", 1, tc)))
                    tick(vec, vec.tensor_scalar(out=col("rsx1n", 1, tc), in0=col("tmpc"),
                                                scalar1=-1.0, scalar2=None, op0=Alu.mult))
                    vec.tensor_scalar(out=ub[:], in0=xh2[:],
                                      scalar1=col("sclx", 1, tc), scalar2=128.5,
                                      op0=Alu.mult, op1=Alu.add)
                    vec.tensor_copy(out=vvb[:], in_=ub[:])
                    for j in range(7, -1, -1):
                        vec.tensor_scalar(out=bb[:], in0=vvb[:], scalar1=float(2 ** j),
                                          scalar2=float(2 ** j), op0=Alu.is_ge,
                                          op1=Alu.mult)
                        vec.tensor_tensor(out=vvb[:], in0=vvb[:], in1=bb[:],
                                          op=Alu.subtract)
                    vec.tensor_tensor(out=bb[:], in0=ub[:], in1=vvb[:], op=Alu.subtract)
                    mark(vec.tensor_scalar(out=xqbf[:], in0=bb[:], scalar1=-128.0,
                                           scalar2=None, op0=Alu.add),
                         "sDV", f"f_u{tc}")

                for qd in range(4):
                    @block.tensor
                    def _(te, tc=tc, qd=qd):
                        wt(te, f"f_u{tc}")
                        gid = tc * 4 + qd
                        if gid >= 2:
                            wt(te, f"f_xqe{(gid - 2) // 4}_{(gid - 2) % 4}")
                        for u in range(4):
                            dc = qd * 4 + u
                            _li = te.transpose(out=psq[qd % 2][:, 128 * u:128 * (u + 1)],
                                               in_=xqbf[:, 128 * dc:128 * (dc + 1)],
                                               identity=identb[:])
                        mark(_li, "sPE", f"f_xqt{tc}_{qd}")

                    @block.vector
                    def _(vec, tc=tc, qd=qd):
                        wt(vec, f"f_xqt{tc}_{qd}")
                        dst = xqT[:, 2048 * tc + 512 * qd:2048 * tc + 512 * (qd + 1)]
                        mark(vec.tensor_copy(out=dst, in_=psq[qd % 2][:]),
                             "sDV", f"f_xqe{tc}_{qd}")
            sta.close()
            barrier()

        if phases_upto >= 7:
            stb = ExitStack()
            wa = [sbT(stb, "g_wa", [128, D]), sbT(stb, "g_wb", [128, D])]
            b1 = sbT(stb, "g_b1", [128, D])
            tnn = sbT(stb, "g_tnn", [128, D], BF16)
            slab = sbT(stb, "g_slab", [128, 8192], BF16)
            swrow = sbT(stb, "g_swrow", [1, 512])
            sw1b = sbT(stb, "g_sw1b", [128, 512])
            hpre = [sbT(stb, "g_hpre0", [128, 512]), sbT(stb, "g_hpre1", [128, 512])]
            hsq = sbT(stb, "g_hsq", [128, 512])
            x3b = sbT(stb, "g_x3", [128, 512])
            thb = [sbT(stb, "g_th0", [128, 512]), sbT(stb, "g_th1", [128, 512])]
            v1g = [sbT(stb, "g_v1a", [128, 512]), sbT(stb, "g_v1b", [128, 512])]
            tmp5 = sbT(stb, "g_tmp5", [128, 512])
            psH = [psT(stb, "g_psH0", [128, 512]), psT(stb, "g_psH1", [128, 512])]
            psW = [psT(stb, "g_psW0", [128, 512], BF16),
                   psT(stb, "g_psW1", [128, 512], BF16)]
            psR = psT(stb, "g_psR", [128, 512])
            psB = psT(stb, "g_psB", [128, 512])

            @block.vector
            def _(vec):
                mark(vec.memset(col("ghmax", 16), 0.0), "sDV", "g_ghinit")

            for ht in range(16):
                for c in range(4):
                    @block.sync
                    def _(sy, ht=ht, c=c):
                        gid = ht * 4 + c
                        if gid >= 2:
                            wt(sy, f"g_tnn{gid - 2}")
                        r0 = 512 * ht + 128 * c
                        dmax(sy, "sLD", wa[gid % 2][:], w1[r0:r0 + 128, :],
                             label=f"g_w{gid}")

                    @block.vector
                    def _(vec, ht=ht, c=c):
                        gid = ht * 4 + c
                        wt(vec, f"g_w{gid}")
                        tick(vec, vec.reduce_sum(out=col("tm"), in_=wa[gid % 2][:],
                                                 axis=AX, apply_absolute_value=True))
                        tick(vec, vec.tensor_scalar(out=col("tmpa"), in0=col("tm"),
                                                    scalar1=1.0 / 2048, scalar2=None,
                                                    op0=Alu.mult))
                        tick(vec, vec.tensor_scalar(out=col("tmpb"), in0=col("tmpa"),
                                                    scalar1=1e-5, scalar2=None,
                                                    op0=Alu.max))
                        tick(vec, vec.tensor_scalar(out=col("swq", 1, c), in0=col("tmpb"),
                                                    scalar1=0.5, scalar2=None,
                                                    op0=Alu.mult))
                        tick(vec, vec.tensor_scalar(out=col("tmpd"), in0=col("swq", 1, c),
                                                    scalar1=-1.0, scalar2=None,
                                                    op0=Alu.mult))
                        vec.tensor_scalar(out=b1[:], in0=wa[gid % 2][:],
                                          scalar1=col("swq", 1, c), scalar2=None,
                                          op0=Alu.is_gt)
                        mark(vec.scalar_tensor_tensor(out=tnn[:], in0=wa[gid % 2][:],
                                                      scalar=col("tmpd"), in1=b1[:],
                                                      op0=Alu.is_lt, op1=Alu.subtract),
                             "sDV", f"g_tnn{gid}")

                    for qd in range(4):
                        @block.tensor
                        def _(te, ht=ht, c=c, qd=qd):
                            gid = (ht * 4 + c) * 4 + qd
                            wt(te, f"g_tnn{ht * 4 + c}")
                            if gid >= 2:
                                wt(te, f"g_sle{gid - 2}")
                            for u in range(4):
                                dc = qd * 4 + u
                                _li = te.transpose(out=psW[gid % 2][:, 128 * u:128 * (u + 1)],
                                                   in_=tnn[:, 128 * dc:128 * (dc + 1)],
                                                   identity=identb[:])
                            mark(_li, "sPE", f"g_slt{gid}")

                        @block.vector
                        def _(vec, ht=ht, c=c, qd=qd):
                            gid = (ht * 4 + c) * 4 + qd
                            wt(vec, f"g_slt{gid}")
                            dst = slab[:].rearrange(
                                "p (dc c4 cc) -> p dc c4 cc", dc=16, c4=4)[
                                :, 4 * qd:4 * qd + 4, c, :]
                            mark(vec.tensor_copy(
                                out=dst,
                                in_=psW[gid % 2][:].rearrange("p (u cc) -> p u cc", u=4)),
                                "sDV", f"g_sle{gid}")

                    @block.tensor
                    def _(te, ht=ht, c=c):
                        wt(te, f"g_tnn{ht * 4 + c}")
                        if lab.get(f"g_swe{ht}_{c - 1}") is not None:
                            wt(te, f"g_swe{ht}_{c - 1}")
                        mark(te.transpose(out=psR[0:1, 128 * c:128 * (c + 1)],
                                          in_=col("swq", 1, c), identity=ident[:]),
                             "sPE", f"g_swt{ht}_{c}")

                    @block.vector
                    def _(vec, ht=ht, c=c):
                        wt(vec, f"g_swt{ht}_{c}")
                        mark(vec.tensor_copy(out=swrow[0:1, 128 * c:128 * (c + 1)],
                                             in_=psR[0:1, 128 * c:128 * (c + 1)]),
                             "sDV", f"g_swe{ht}_{c}")

                @block.tensor
                def _(te, ht=ht):
                    wt(te, f"g_swe{ht}_3")
                    if lab.get(f"g_swbe{ht - 1}") is not None:
                        wt(te, f"g_swbe{ht - 1}")
                    mark(te.matmul(out=psB[:], lhsT=ones1[:], rhs=swrow[0:1, :],
                                   start=True, stop=True), "sPE", f"g_swb{ht}")

                @block.vector
                def _(vec, ht=ht):
                    wt(vec, f"g_swb{ht}")
                    mark(vec.tensor_scalar(out=sw1b[:], in0=psB[:], scalar1=2.0,
                                           scalar2=None, op0=Alu.mult),
                         "sDV", f"g_swbe{ht}")

                for tc in range(16):
                    @block.tensor
                    def _(te, ht=ht, tc=tc):
                        gid = ht * 16 + tc
                        wt(te, f"g_sle{(ht * 4 + 3) * 4 + 3}")
                        wt(te, f"f_xqe{tc}_3")
                        if gid >= 2:
                            wt(te, f"g_hev{gid - 2}")
                        for dc in range(16):
                            _li = te.matmul(out=psH[gid % 2][:],
                                            lhsT=xqT[:, 2048 * tc + 128 * dc:2048 * tc + 128 * (dc + 1)],
                                            rhs=slab[:, 512 * dc:512 * (dc + 1)],
                                            start=(dc == 0), stop=(dc == 15))
                        mark(_li, "sPE", f"g_hmm{gid}")

                    @block.vector
                    def _(vec, ht=ht, tc=tc):
                        gid = ht * 16 + tc
                        wt(vec, f"g_hmm{gid}")
                        wt(vec, f"g_swbe{ht}")
                        if gid > 1:
                            wt(vec, f"g_v1{gid - 2}")
                        mark(vec.scalar_tensor_tensor(out=hpre[gid % 2][:],
                                                      in0=psH[gid % 2][:],
                                                      scalar=col("rsx1n", 1, tc),
                                                      in1=sw1b[:], op0=Alu.mult,
                                                      op1=Alu.mult),
                             "sDV", f"g_hev{gid}")

                    @block.vector
                    def _(vec, ht=ht, tc=tc):
                        gid = ht * 16 + tc
                        wt(vec, f"g_hev{gid}")
                        vec.tensor_tensor(out=hsq[:], in0=hpre[gid % 2][:],
                                          in1=hpre[gid % 2][:], op=Alu.mult)
                        vec.tensor_scalar(out=tmp5[:], in0=hsq[:], scalar1=C1_HI,
                                          scalar2=GC_HI, op0=Alu.mult, op1=Alu.add)
                        vec.tensor_scalar(out=x3b[:], in0=hsq[:], scalar1=C1_LO,
                                          scalar2=GC_LO, op0=Alu.mult, op1=Alu.add)
                        vec.tensor_tensor(out=tmp5[:], in0=tmp5[:], in1=x3b[:],
                                          op=Alu.add)
                        mark(vec.tensor_tensor(out=x3b[:], in0=tmp5[:],
                                               in1=hpre[gid % 2][:], op=Alu.mult),
                             "sDV", f"g_x3{gid}")

                    @block.scalar
                    def _(sc, ht=ht, tc=tc):
                        gid = ht * 16 + tc
                        wt(sc, f"g_x3{gid}")
                        mark(sc.activation(out=thb[gid % 2][:], in_=x3b[:],
                                           func=AF.Tanh), "sAC", f"g_th{gid}")

                    def emit_v1(gid):
                        ht_, tc_ = gid // 16, gid % 16

                        @block.vector
                        def _(vec, gid=gid, ht_=ht_, tc_=tc_):
                            wt(vec, f"g_th{gid}")
                            if gid >= 2:
                                wt(vec, f"g_v1s{gid - 2}")
                            mark(vec.scalar_tensor_tensor(out=v1g[gid % 2][:],
                                                          in0=thb[gid % 2][:],
                                                          scalar=1.0,
                                                          in1=hpre[gid % 2][:],
                                                          op0=Alu.add, op1=Alu.mult),
                                 "sDV", f"g_v1{gid}")
                            tick(vec, vec.reduce_max(out=col("tm"), in_=v1g[gid % 2][:],
                                                     axis=AX,
                                                     apply_absolute_value=True))
                            mark(vec.tensor_tensor(out=col("ghmax", 1, tc_),
                                                   in0=col("ghmax", 1, tc_),
                                                   in1=col("tm"), op=Alu.max),
                                 "sDV", f"g_gm{gid}")

                        @block.gpsimd
                        def _(gp, gid=gid, ht_=ht_, tc_=tc_):
                            wt(gp, f"g_gm{gid}")
                            dmax(gp, "sGP", v1_scr[tc_][:, 512 * ht_:512 * (ht_ + 1)],
                                 v1g[gid % 2][:], label=f"g_v1s{gid}")

                    if ht * 16 + tc >= 1:
                        emit_v1(ht * 16 + tc - 1)
                    if ht == 15 and tc == 15:
                        emit_v1(255)

            @block.vector
            def _(vec):
                wt(vec, f"g_gm{16 * 16 - 1}")
                tick(vec, vec.tensor_scalar(out=col("tmpa", 16), in0=col("ghmax", 16),
                                            scalar1=0.5, scalar2=None, op0=Alu.mult))
                tick(vec, vec.tensor_scalar(out=col("tmpb", 16), in0=col("tmpa", 16),
                                            scalar1=1e-5, scalar2=None, op0=Alu.max))
                tick(vec, vec.reciprocal(out=col("tmpc", 16), in_=col("tmpb", 16)))
                tick(vec, vec.tensor_scalar(out=col("tmpd", 16), in0=col("tmpc", 16),
                                            scalar1=127.0, scalar2=None, op0=Alu.mult))
                tick(vec, vec.tensor_scalar(out=col("s05", 16), in0=col("tmpd", 16),
                                            scalar1=0.5, scalar2=None, op0=Alu.mult))
                tick(vec, vec.reciprocal(out=col("tmpa", 16), in_=col("tmpd", 16)))
                mark(vec.tensor_scalar(out=col("rs2neg", 16), in0=col("tmpa", 16),
                                       scalar1=-1.0, scalar2=None, op0=Alu.mult),
                     "sDV", "g_scl2")
            stb.close()
            barrier()
            st4.close()

        if phases_upto >= 8:
            stc = ExitStack()
            va = [sbT(stc, "h_va0", [128, HID]), sbT(stc, "h_va1", [128, HID])]
            ub = sbT(stc, "h_u", [128, HID])
            vvb = sbT(stc, "h_vv", [128, HID])
            bb = sbT(stc, "h_bb", [128, HID])
            nqb = sbT(stc, "h_nq", [128, HID], BF16)
            hqe = sbT(stc, "h_hqe", [128, 8192], BF16)
            psq = [psT(stc, "h_ps0", [128, 512], BF16), psT(stc, "h_ps1", [128, 512], BF16)]
            for tc in range(16):
                @block.sync
                def _(sy, tc=tc):
                    if tc > 1:
                        wt(sy, f"h_nq{tc - 2}")
                    dmax(sy, "sLD", va[tc % 2][:], v1_scr[tc][:, :], label=f"h_v1{tc}")

                @block.vector
                def _(vec, tc=tc):
                    wt(vec, f"h_v1{tc}")
                    if tc == 0:
                        wt(vec, "g_scl2")
                    if tc > 0:
                        wt(vec, f"h_hqs{tc - 1}")
                    vec.tensor_scalar(out=ub[:], in0=va[tc % 2][:],
                                      scalar1=col("s05", 1, tc), scalar2=128.5,
                                      op0=Alu.mult, op1=Alu.add)
                    vec.tensor_copy(out=vvb[:], in_=ub[:])
                    for j in range(7, -1, -1):
                        vec.tensor_scalar(out=bb[:], in0=vvb[:], scalar1=float(2 ** j),
                                          scalar2=float(2 ** j), op0=Alu.is_ge,
                                          op1=Alu.mult)
                        vec.tensor_tensor(out=vvb[:], in0=vvb[:], in1=bb[:],
                                          op=Alu.subtract)
                    vec.tensor_tensor(out=bb[:], in0=ub[:], in1=vvb[:], op=Alu.subtract)
                    mark(vec.tensor_scalar(out=nqb[:], in0=bb[:], scalar1=-128.0,
                                           scalar2=None, op0=Alu.add),
                         "sDV", f"h_nq{tc}")

                for qd in range(16):
                    @block.tensor
                    def _(te, tc=tc, qd=qd):
                        gid = tc * 16 + qd
                        wt(te, f"h_nq{tc}")
                        if gid >= 2:
                            wt(te, f"h_te{gid - 2}")
                        for u in range(4):
                            hc = qd * 4 + u
                            _li = te.transpose(out=psq[qd % 2][:, 128 * u:128 * (u + 1)],
                                               in_=nqb[:, 128 * hc:128 * (hc + 1)],
                                               identity=identb[:])
                        mark(_li, "sPE", f"h_tt{gid}")

                    @block.vector
                    def _(vec, tc=tc, qd=qd):
                        gid = tc * 16 + qd
                        wt(vec, f"h_tt{gid}")
                        mark(vec.tensor_copy(out=hqe[:, 512 * qd:512 * (qd + 1)],
                                             in_=psq[qd % 2][:]),
                             "sDV", f"h_te{gid}")

                @block.gpsimd
                def _(gp, tc=tc):
                    wt(gp, f"h_te{tc * 16 + 15}")
                    dmax(gp, "sGP",
                         hqT_scr[tc].rearrange("hc p t -> p hc t"),
                         hqe[:].rearrange("p (hc t) -> p hc t", t=128),
                         label=f"h_hqs{tc}")
            stc.close()
            barrier()

        if phases_upto >= 9:
            std = ExitStack()
            w2a = sbT(std, "m_w2a", [128, HID])
            b1 = sbT(std, "m_b1", [128, HID])
            tnn = sbT(std, "m_tnn", [128, HID], BF16)
            slab2 = sbT(std, "m_slab2", [128, 32768], BF16)
            hqt = [sbT(std, "m_hqt0", [128, 8192], BF16),
                   sbT(std, "m_hqt1", [128, 8192], BF16)]
            xsl = [sbT(std, "m_xsl0", [128, 512]), sbT(std, "m_xsl1", [128, 512])]
            otb = [sbT(std, "m_ot0", [128, 512]), sbT(std, "m_ot1", [128, 512])]
            e1b = sbT(std, "m_e1", [128, 512])
            swrow = sbT(std, "m_swrow", [1, 512])
            sw2b = sbT(std, "m_sw2b", [128, 512])
            psF = [psT(std, "m_psF0", [128, 512]), psT(std, "m_psF1", [128, 512])]
            psW = [psT(std, "m_psW0", [128, 512], BF16),
                   psT(std, "m_psW1", [128, 512], BF16)]
            psR = psT(std, "m_psR", [128, 512])
            psB = psT(std, "m_psB", [128, 512])
            for dt in range(4):
                for c in range(4):
                    @block.sync
                    def _(sy, dt=dt, c=c):
                        gid = dt * 4 + c
                        if gid > 0:
                            wt(sy, f"m_tnn{gid - 1}")
                        r0 = 512 * dt + 128 * c
                        dmax(sy, "sLD", w2a[:], w2[r0:r0 + 128, :],
                             label=f"m_w{gid}")

                    @block.vector
                    def _(vec, dt=dt, c=c):
                        gid = dt * 4 + c
                        wt(vec, f"m_w{gid}")
                        tick(vec, vec.reduce_sum(out=col("tm"), in_=w2a[:], axis=AX,
                                                 apply_absolute_value=True))
                        tick(vec, vec.tensor_scalar(out=col("tmpa"), in0=col("tm"),
                                                    scalar1=1.0 / 8192, scalar2=None,
                                                    op0=Alu.mult))
                        tick(vec, vec.tensor_scalar(out=col("tmpb"), in0=col("tmpa"),
                                                    scalar1=1e-5, scalar2=None,
                                                    op0=Alu.max))
                        tick(vec, vec.tensor_scalar(out=col("swq", 1, c), in0=col("tmpb"),
                                                    scalar1=0.5, scalar2=None,
                                                    op0=Alu.mult))
                        tick(vec, vec.tensor_scalar(out=col("tmpd"), in0=col("swq", 1, c),
                                                    scalar1=-1.0, scalar2=None,
                                                    op0=Alu.mult))
                        vec.tensor_scalar(out=b1[:], in0=w2a[:],
                                          scalar1=col("swq", 1, c), scalar2=None,
                                          op0=Alu.is_gt)
                        mark(vec.scalar_tensor_tensor(out=tnn[:], in0=w2a[:],
                                                      scalar=col("tmpd"), in1=b1[:],
                                                      op0=Alu.is_lt, op1=Alu.subtract),
                             "sDV", f"m_tnn{gid}")

                    for qd in range(16):
                        @block.tensor
                        def _(te, dt=dt, c=c, qd=qd):
                            gid = (dt * 4 + c) * 16 + qd
                            wt(te, f"m_tnn{dt * 4 + c}")
                            if gid >= 2:
                                wt(te, f"m_sle{gid - 2}")
                            for u in range(4):
                                hc = qd * 4 + u
                                _li = te.transpose(out=psW[gid % 2][:, 128 * u:128 * (u + 1)],
                                                   in_=tnn[:, 128 * hc:128 * (hc + 1)],
                                                   identity=identb[:])
                            mark(_li, "sPE", f"m_slt{gid}")

                        @block.vector
                        def _(vec, dt=dt, c=c, qd=qd):
                            gid = (dt * 4 + c) * 16 + qd
                            wt(vec, f"m_slt{gid}")
                            dst = slab2[:].rearrange(
                                "p (hc c4 cc) -> p hc c4 cc", hc=64, c4=4)[
                                :, 4 * qd:4 * qd + 4, c, :]
                            mark(vec.tensor_copy(
                                out=dst,
                                in_=psW[gid % 2][:].rearrange("p (u cc) -> p u cc", u=4)),
                                "sDV", f"m_sle{gid}")

                    @block.tensor
                    def _(te, dt=dt, c=c):
                        wt(te, f"m_tnn{dt * 4 + c}")
                        if lab.get(f"m_swe{dt}_{c - 1}") is not None:
                            wt(te, f"m_swe{dt}_{c - 1}")
                        mark(te.transpose(out=psR[0:1, 128 * c:128 * (c + 1)],
                                          in_=col("swq", 1, c), identity=ident[:]),
                             "sPE", f"m_swt{dt}_{c}")

                    @block.vector
                    def _(vec, dt=dt, c=c):
                        wt(vec, f"m_swt{dt}_{c}")
                        mark(vec.tensor_copy(out=swrow[0:1, 128 * c:128 * (c + 1)],
                                             in_=psR[0:1, 128 * c:128 * (c + 1)]),
                             "sDV", f"m_swe{dt}_{c}")

                @block.tensor
                def _(te, dt=dt):
                    wt(te, f"m_swe{dt}_3")
                    if lab.get(f"m_swbe{dt - 1}") is not None:
                        wt(te, f"m_swbe{dt - 1}")
                    mark(te.matmul(out=psB[:], lhsT=ones1[:], rhs=swrow[0:1, :],
                                   start=True, stop=True), "sPE", f"m_swb{dt}")

                @block.vector
                def _(vec, dt=dt):
                    wt(vec, f"m_swb{dt}")
                    mark(vec.tensor_scalar(out=sw2b[:], in0=psB[:], scalar1=2.0,
                                           scalar2=None, op0=Alu.mult),
                         "sDV", f"m_swbe{dt}")

                for tc in range(16):
                    @block.sync
                    def _(sy, dt=dt, tc=tc):
                        gid = dt * 16 + tc
                        if gid > 1:
                            wt(sy, f"m_e1{gid - 2}")
                        dmax(sy, "sLD", hqt[gid % 2][:].rearrange("p (hc t) -> p hc t", t=128),
                             hqT_scr[tc].rearrange("hc p t -> p hc t"))
                        dmax(sy, "sLD", xsl[gid % 2][:],
                             x2_scr[128 * tc:128 * (tc + 1), 512 * dt:512 * (dt + 1)],
                             label=f"m_ld{gid}")

                    @block.tensor
                    def _(te, dt=dt, tc=tc):
                        gid = dt * 16 + tc
                        wt(te, f"m_ld{gid}")
                        wt(te, f"m_sle{(dt * 4 + 3) * 16 + 15}")
                        if gid >= 2:
                            wt(te, f"m_e1{gid - 2}")
                        for hc in range(64):
                            _li = te.matmul(out=psF[gid % 2][:],
                                            lhsT=hqt[gid % 2][:, 128 * hc:128 * (hc + 1)],
                                            rhs=slab2[:, 512 * hc:512 * (hc + 1)],
                                            start=(hc == 0), stop=(hc == 63))
                        mark(_li, "sPE", f"m_mm{gid}")

                    @block.vector
                    def _(vec, dt=dt, tc=tc):
                        gid = dt * 16 + tc
                        wt(vec, f"m_mm{gid}")
                        wt(vec, f"m_swbe{dt}")
                        if gid >= 2:
                            wt(vec, f"m_os{gid - 2}")
                        vec.scalar_tensor_tensor(out=e1b[:], in0=psF[gid % 2][:],
                                                 scalar=col("rs2neg", 1, tc),
                                                 in1=sw2b[:], op0=Alu.mult,
                                                 op1=Alu.mult)
                        mark(vec.tensor_tensor(out=otb[gid % 2][:], in0=e1b[:],
                                               in1=xsl[gid % 2][:], op=Alu.add),
                             "sDV", f"m_e1{gid}")

                    @block.gpsimd
                    def _(gp, dt=dt, tc=tc):
                        gid = dt * 16 + tc
                        wt(gp, f"m_e1{gid}")
                        dmax(gp, "sGP",
                             o_q[128 * tc:128 * (tc + 1), 512 * dt:512 * (dt + 1)],
                             otb[gid % 2][:], label=f"m_os{gid}")
            std.close()

        try:
            st4.close()
        except Exception:
            pass

        for sname in ("sLD", "sGP", "sSC"):
            fin = cnt.get(sname, 0)
            if fin:
                @block.gpsimd
                def _(gp, sname=sname, fin=fin):
                    gp.wait_ge(sem[sname], fin)

    return nc


def stage_inputs(inputs, core):
    b, g = core // 2, core % 2
    x = np.asarray(inputs["L0_attn_x"])
    m = {
        "x_full": np.ascontiguousarray(x[b]),
        "x_q": np.ascontiguousarray(x[b][qperm_rows(g)]),
        "Wq": np.asarray(inputs["L0_attn_Wq"]),
        "Wk": np.asarray(inputs["L0_attn_Wk"]),
        "Wv": np.asarray(inputs["L0_attn_Wv"]),
        "Wo": np.asarray(inputs["L0_attn_Wo"]),
        "bq_t": np.ascontiguousarray(np.asarray(inputs["L0_attn_bq"]).reshape(16, 128).T),
        "bk_t": np.ascontiguousarray(np.asarray(inputs["L0_attn_bk"]).reshape(16, 128).T),
        "bv_row": np.asarray(inputs["L0_attn_bv"]).reshape(1, D),
        "bo_row": np.asarray(inputs["L0_attn_bo"]).reshape(1, D),
        "w1": np.asarray(inputs["L0_ffn_w1"]),
        "w2": np.asarray(inputs["L0_ffn_w2"]),
        "alibi": alibi_table(),
        "maskc": mask_consts(g),
        "identd": np.eye(128, dtype=np.float32),
    }
    return {k: np.ascontiguousarray(np.asarray(v, dtype=np.float32)) for k, v in m.items()}


def kernel(**inputs):
    nc = build_program()
    in_maps = [stage_inputs(inputs, c) for c in range(8)]
    res = run_bass_kernel_spmd(nc, in_maps, core_ids=list(range(8)))
    out = np.empty((4, S, D), np.float32)
    for c in range(8):
        b, g = c // 2, c % 2
        out[b][qperm_rows(g)] = np.asarray(res.results[c]["o_q"])
    return out
